# version 01
# baseline (speedup 1.0000x reference)
import numpy as np
import ml_dtypes
from contextlib import ExitStack

import concourse.bass as bass
import concourse.mybir as mybir
from concourse.bass_utils import run_bass_kernel_spmd

F32 = mybir.dt.float32
BF16 = mybir.dt.bfloat16
AF = mybir.ActivationFunctionType
OP = mybir.AluOpType
AX = mybir.AxisListType
BF = ml_dtypes.bfloat16

ENGINES = ("sync", "gpsimd", "tensor", "vector", "scalar")

D = 2048
E = 4096
R = 128
NST = 16
KCONV = 4
NHEAD = 16
HDIM = 128
BATCH = 2
SEQ = 4096
TOKENS = BATCH * SEQ
NCORE = 8
TC = TOKENS // NCORE
ES = E // NCORE
IEXP = 1408
ISH = 2816
NEXP = 8


class Sem:
    __slots__ = ("h", "n")

    def __init__(self, h):
        self.h = h
        self.n = 0


class Prog:

    def __init__(self, nc, stack):
        self.nc = nc
        self.stack = stack
        self.q = {e: [] for e in ENGINES}
        self.esem = {e: self.sem(f"es_{e}") for e in ("tensor", "vector", "scalar", "gpsimd")}
        self.waited = {e: {} for e in ENGINES}

    def sem(self, name):
        return Sem(self.stack.enter_context(self.nc.semaphore(name)))

    def init_consts(self, vals, sem):
        arr = np.tile(np.asarray(vals, np.float32)[None, :], (128, 1))
        dr = self.nc.inline_tensor(arr, name="cconsts")
        tile = self.stack.enter_context(
            self.nc.sbuf_tensor("cconsts_sb", [128, len(vals)], F32))
        tok = self.dma("sync", sem, tile[:, :], dr.ap())
        for j, v in enumerate(vals):
            self.nc.const_aps.register(F32, float(v), tile[:, j:j + 1])
        for e in ("tensor", "vector", "scalar", "gpsimd"):
            self.wait_only(e, tok)
        return tok

    def _wait(self, eng, tok):
        if tok is None:
            return
        sem, val = tok
        if val <= 0:
            return
        w = self.waited[eng]
        if w.get(sem, 0) >= val:
            return
        w[sem] = val
        self.q[eng].append(lambda Eng, s=sem, v=val: Eng.wait_ge(s.h, v))

    @staticmethod
    def _reduce(waits):
        best = {}
        for t in waits:
            if t is None:
                continue
            s, v = t
            if best.get(id(s), (None, -1))[1] < v:
                best[id(s)] = (s, v)
        return best.values()

    def op(self, eng, fn, waits=()):
        for t in self._reduce(waits):
            self._wait(eng, t)
        s = self.esem[eng]
        s.n += 1

        def emit(Eng, fn=fn, s=s):
            fn(Eng).then_inc(s.h, 1)

        self.q[eng].append(emit)
        return (s, s.n)

    def raw(self, eng, fn, waits=()):
        for t in self._reduce(waits):
            self._wait(eng, t)
        self.q[eng].append(fn)

    def dma(self, eng, sem, out, in_, waits=()):
        for t in self._reduce(waits):
            self._wait(eng, t)
        sem.n += 16

        def emit(Eng, out=out, in_=in_, sem=sem):
            Eng.dma_start(out=out, in_=in_).then_inc(sem.h, 16)

        self.q[eng].append(emit)
        return (sem, sem.n)

    def wait_only(self, eng, tok):
        self._wait(eng, tok)

    def tokens(self):
        return [(s, s.n) for s in self.esem.values() if s.n > 0]

    def barrier(self, toks):
        for e in ENGINES:
            for t in toks:
                self.wait_only(e, t)

    def pool_reg(self, val):
        cell = {}

        def setup(Eng, cell=cell, val=val):
            r = Eng.alloc_register("fill_shared")
            Eng.reg_mov(r, val)
            cell["r"] = r

        self.q["gpsimd"].append(setup)
        return cell

    def run(self, block):
        for e in ENGINES:
            def body(Eng, ops=tuple(self.q[e])):
                for f in ops:
                    f(Eng)
            getattr(block, e)(body)


def _spans(n, w=512):
    out = []
    off = 0
    while off < n:
        out.append((off, min(w, n - off)))
        off += min(w, n - off)
    return out


def _kt_view(ap2d, p=128):
    return ap2d.rearrange("(kt p) m -> p kt m", p=p)


class WStream:

    def __init__(self, p, name, kdim, mtile=128, nbuf=2):
        self.p = p
        self.kt_n = kdim // 128
        self.mtile = mtile
        self.bufs = [p.stack.enter_context(
            p.nc.sbuf_tensor(f"{name}{i}", [128, self.kt_n * mtile], BF16))
            for i in range(nbuf)]
        self.sems = [p.sem(f"{name}_s{i}") for i in range(nbuf)]
        self.nbuf = nbuf
        self.uses = 0
        self.last_cons = [None] * nbuf

    def load(self, src3d, kt_n=None):
        k = kt_n or self.kt_n
        assert k <= self.kt_n
        i = self.uses % self.nbuf
        self.uses += 1
        dst = self.bufs[i][:, 0:k * self.mtile].rearrange("p (kt m) -> p kt m", kt=k)
        tok = self.p.dma("gpsimd", self.sems[i], dst, src3d,
                         waits=[self.last_cons[i]] if self.last_cons[i] else [])
        return i, tok

    def consumed(self, i, tok):
        self.last_cons[i] = tok

    def tile(self, i, kt):
        return self.bufs[i][:, kt * self.mtile:(kt + 1) * self.mtile]


def proj_stream(p, ws, wt3d, mts, spans, rhs_fn, banks, bstate, evac_fn,
                rhs_waits=(), extra_mm_waits=()):
    kt_n = ws.kt_n
    nb = len(banks)
    bi = 0
    for m_idx in mts:
        bufi, wtok = ws.load(wt3d[:, :, m_idx * ws.mtile:(m_idx + 1) * ws.mtile])
        mm_last = None
        for si, (off, w) in enumerate(spans):
            b = bi % nb
            bi += 1
            waits = [wtok] + list(rhs_waits) + list(extra_mm_waits)
            if bstate.get(b) is not None:
                waits.append(bstate[b])
            for kt in range(kt_n):
                fn = (lambda E, ps=banks[b], wt=ws.tile(bufi, kt), r=rhs_fn(kt, off, w),
                      st=(kt == 0), sp=(kt == kt_n - 1), w_=w:
                      E.matmul(ps[:, :w_], wt, r, start=st, stop=sp))
                if kt == kt_n - 1:
                    mm_last = p.op("tensor", fn, waits=waits)
                else:
                    p.raw("tensor", fn, waits=waits)
                waits = []
            rel = evac_fn(m_idx, si, off, w, banks[b], mm_last)
            bstate[b] = rel
        ws.consumed(bufi, mm_last)


def layernorm_fm(p, x_sb, xn_sb, sq_sc, rows, banks, consts, ncols, kt_n,
                 x_ready, bc_banks=(6, 7)):
    spans = []
    off = 0
    while off < ncols:
        w = min(512, ncols - off)
        spans.append((off, w))
        off += w
    nsp = len(spans)
    assert nsp <= 3
    oc, orow = consts["ones_col"], consts["ones_row32"]
    f32_sc = consts["f32_sc"]

    sum_tok = [None] * nsp
    sq_guard = None
    for kt in range(kt_n):
        xcol = x_sb[:, kt * ncols:(kt + 1) * ncols]
        tsq = p.op("scalar", lambda E, o=sq_sc[:, :ncols], i=xcol:
                   E.activation(o, i, AF.Square),
                   waits=[x_ready] + ([sq_guard] if sq_guard else []))
        for si, (o, w) in enumerate(spans):
            p.raw("tensor", lambda E, b=banks[si], i=x_sb[:, kt * ncols + o: kt * ncols + o + w],
                  st=(kt == 0), sp=(kt == kt_n - 1), w_=w, oc=oc:
                  E.matmul(b[0:1, :w_], oc, i, start=st, stop=sp),
                  waits=[x_ready] if (kt == 0 and si == 0) else [])
            fn = (lambda E, b=banks[3 + si], i=sq_sc[:, o:o + w], st=(kt == 0),
                  sp=(kt == kt_n - 1), w_=w, oc=oc:
                  E.matmul(b[0:1, :w_], oc, i, start=st, stop=sp))
            if si == nsp - 1:
                tok = p.op("tensor", fn, waits=[tsq])
                sq_guard = tok
                if kt == kt_n - 1:
                    sum_tok = [tok] * nsp
            else:
                p.raw("tensor", fn, waits=[tsq])

    sx = rows[:, 0 * ncols:1 * ncols]
    ssq = rows[:, 1 * ncols:2 * ncols]
    mean = rows[:, 2 * ncols:3 * ncols]
    m2 = rows[:, 3 * ncols:4 * ncols]
    var = rows[:, 4 * ncols:5 * ncols]
    rstd = rows[:, 5 * ncols:6 * ncols]
    ev = None
    for si, (o, w) in enumerate(spans):
        ev = p.op("vector", lambda E, o1=sx[:, o:o + w], i=banks[si][0:1, :w]:
                  E.tensor_copy(o1, i), waits=[sum_tok[si]] + ([ev] if ev else []))
        ev = p.op("vector", lambda E, o1=ssq[:, o:o + w], i=banks[3 + si][0:1, :w]:
                  E.tensor_copy(o1, i), waits=[ev])
    tm = p.op("vector", lambda E: E.tensor_scalar_mul(mean, sx, 1.0 / D), waits=[ev])
    tm2 = p.op("scalar", lambda E: E.activation(m2, mean, AF.Square), waits=[tm])
    tv = p.op("vector", lambda E: E.scalar_tensor_tensor(
        var, ssq, 1.0 / D, m2, OP.mult, OP.subtract), waits=[tm2, tm])
    tsd = p.op("scalar", lambda E: E.activation(rstd, var, AF.Sqrt, bias=1e-5),
               waits=[tv])
    trs = p.op("vector", lambda E: E.reciprocal(rstd, rstd), waits=[tsd])

    bmu = banks[bc_banks[0]]
    brs = banks[bc_banks[1]]
    prev_bc = None
    prev_sc = None
    xn_tok = None
    for si, (o, w) in enumerate(spans):
        tbm = p.op("tensor", lambda E, r=mean[:, o:o + w], w_=w:
                   E.matmul(bmu[:, :w_], orow, r, start=True, stop=True),
                   waits=[tm] + ([prev_bc] if prev_bc else []))
        tbr = p.op("tensor", lambda E, r=rstd[:, o:o + w], w_=w:
                   E.matmul(brs[:, :w_], orow, r, start=True, stop=True),
                   waits=[trs] + ([prev_bc] if prev_bc else []))
        for kt in range(kt_n):
            xcol = x_sb[:, kt * ncols + o: kt * ncols + o + w]
            ta = p.op("vector", lambda E, o1=f32_sc[:, :w], i=xcol, w_=w:
                      E.scalar_tensor_tensor(o1, i, 1.0, bmu[:, :w_], OP.mult, OP.subtract),
                      waits=[tbm, x_ready] + ([prev_sc] if prev_sc else []))
            xn_tok = p.op("vector", lambda E, o1=xn_sb[:, kt * ncols + o: kt * ncols + o + w],
                          i=f32_sc[:, :w], w_=w:
                          E.tensor_tensor(o1, i, brs[:, :w_], OP.mult),
                          waits=[tbr, ta])
            prev_sc = xn_tok
        prev_bc = xn_tok
    return xn_tok


def build_launch_A():
    nc = bass.Bass("TRN2", target_bir_lowering=False)
    TH = TC + 3
    x0h = nc.dram_tensor("x0h", [D, TH], F32, kind="ExternalInput").ap()
    winT_u = nc.dram_tensor("winT_u", [D, E], F32, kind="ExternalInput").ap()
    winT_g = nc.dram_tensor("winT_g", [D, E], F32, kind="ExternalInput").ap()
    ET0 = E // 128
    convw = nc.dram_tensor("convw", [128, ET0 * KCONV], F32, kind="ExternalInput").ap()
    convb = nc.dram_tensor("convb", [128, ET0], F32, kind="ExternalInput").ap()
    xprojT = nc.dram_tensor("xprojT", [E, 160], F32, kind="ExternalInput").ap()
    dtprojT = nc.dram_tensor("dtprojT", [R, E], F32, kind="ExternalInput").ap()
    dtb = nc.dram_tensor("dtb", [128, ET0], F32, kind="ExternalInput").ap()

    o_xc = nc.dram_tensor("o_xc", [E, TC], BF16, kind="ExternalOutput").ap()
    o_sg = nc.dram_tensor("o_sg", [E, TC], BF16, kind="ExternalOutput").ap()
    o_dt = nc.dram_tensor("o_dt", [E, TC], F32, kind="ExternalOutput").ap()
    o_bc = nc.dram_tensor("o_bc", [32, TC], F32, kind="ExternalOutput").ap()
    o_xn = nc.dram_tensor("o_xn", [D, TC], BF16, kind="ExternalOutput").ap()

    KT = D // 128
    ET = E // 128
    NSLAB = 4
    MTS = ET // NSLAB

    with ExitStack() as stack:
        p = Prog(nc, stack)
        sb = lambda n, s, d: stack.enter_context(nc.sbuf_tensor(n, s, d))
        x_sb = sb("x_sb", [128, KT * TH], BF16)
        xn_sb = sb("xn_sb", [128, KT * TH], BF16)
        sq_sc = sb("sq_sc", [128, TH], BF16)
        rows = sb("rows", [1, 6 * TH], F32)
        f32_sc = sb("f32_sc", [128, 512], F32)
        u_slab = sb("u_slab", [128, MTS * TH], BF16)
        xc_slab = sb("xc_slab", [128, MTS * TC], BF16)
        conv_sc = [sb(f"conv_sc{i}", [128, TC], F32) for i in range(2)]
        convw_sb = sb("convw_sb", [128, ET * KCONV], F32)
        convb_sb = sb("convb_sb", [128, ET], F32)
        dtb_sb = sb("dtb_sb", [128, ET], F32)
        xproj_sb = sb("xproj_sb", [128, ET * 160], BF16)
        dtproj_sb = sb("dtproj_sb", [128, ET * 128], BF16)
        dtlow_sb = sb("dtlow_sb", [128, TC], BF16)
        bc_sb = sb("bc_sb", [32, TC], F32)
        dt_sc = sb("dt_sc", [128, TC], F32)
        sp_scA = sb("sp_scA", [128, 512], F32)
        sp_scB = sb("sp_scB", [128, 512], F32)
        ones_col = sb("ones_col", [128, 1], BF16)
        ones_row32 = sb("ones_row32", [1, 128], F32)
        banks = [stack.enter_context(nc.psum_tensor(f"pb{i}", [128, 512], F32))
                 for i in range(8)]

        s_in = p.sem("s_in")
        s_st = p.sem("s_st")
        s_wx = p.sem("s_wx")
        s_x = p.sem("s_x")
        s_c = p.sem("s_c")
        p.init_consts([1e-5, 1.0, -1.0], s_c)
        ws = WStream(p, "wu", D)

        co = nc.inline_tensor(np.ones((128, 1), dtype=BF), name="ones_col_c")
        cr = nc.inline_tensor(np.ones((1, 128), dtype=np.float32), name="ones_row_c")

        p.dma("gpsimd", s_x, x_sb[:, :].rearrange("p (kt t) -> p kt t", kt=KT),
              _kt_view(x0h))
        p.dma("sync", s_in, convw_sb[:, :], convw)
        p.dma("sync", s_in, convb_sb[:, :], convb)
        p.dma("sync", s_in, dtb_sb[:, :], dtb)
        p.dma("sync", s_in, ones_col[:, :], co.ap())
        p.dma("sync", s_in, ones_row32[:, :], cr.ap())
        p.dma("gpsimd", s_wx, xproj_sb[:, :].rearrange("p (et m) -> p et m", et=ET),
              xprojT.rearrange("(et p) m -> p et m", p=128))
        p.dma("gpsimd", s_wx, dtproj_sb[:, :], dtprojT)
        loads = (s_in, s_in.n)
        wloads = (s_wx, s_wx.n)
        for e in ("tensor", "vector", "scalar", "gpsimd"):
            p.wait_only(e, loads)
        consts = {"ones_col": ones_col[:, :], "ones_row32": ones_row32[:, :],
                  "f32_sc": f32_sc}

        xn_done = layernorm_fm(p, x_sb, xn_sb, sq_sc, rows, banks, consts, TH, KT,
                               x_ready=(s_x, 16), bc_banks=(6, 7))

        spans_h = _spans(TH)
        spans_t = _spans(TC)

        def rhs_xn(kt, off, w):
            return xn_sb[:, kt * TH + off: kt * TH + off + w]

        def rhs_xn_tok(kt, off, w):
            return xn_sb[:, kt * TH + 3 + off: kt * TH + 3 + off + w]

        bstate = {}
        winTu3 = _kt_view(winT_u)
        winTg3 = _kt_view(winT_g)

        conv_done_prev = None
        xdbl_prev = None
        st_xc_prev = None
        xdbl_final = [None, None]
        for s in range(NSLAB):
            evac_toks = {}

            def evac_u(m_idx, si, off, w, ps, mmtok, s=s, evac_toks=evac_toks):
                mt = m_idx - s * MTS
                extra = [conv_done_prev] if conv_done_prev else []
                tok = p.op("scalar",
                           lambda E, o=u_slab[:, mt * TH + off: mt * TH + off + w],
                           i=ps[:, :w]: E.activation(o, i, AF.Copy),
                           waits=[mmtok] + extra)
                evac_toks.setdefault(mt, []).append(tok)
                return tok

            proj_stream(p, ws, winTu3, list(range(s * MTS, (s + 1) * MTS)), spans_h,
                        rhs_xn, banks[:4], bstate, evac_u, rhs_waits=[xn_done])

            conv_toks = []
            for mt in range(MTS):
                et = s * MTS + mt
                ucols = u_slab[:, mt * TH:(mt + 1) * TH]
                t0 = p.op("vector", lambda E, o=conv_sc[0][:, :], i=ucols[:, 0:TC],
                          sc=convw_sb[:, et * KCONV: et * KCONV + 1]:
                          E.tensor_scalar(o, i, sc, None, OP.mult),
                          waits=evac_toks[mt] + [loads] +
                          ([conv_toks[-1]] if conv_toks else
                           ([conv_done_prev] if conv_done_prev else [])))
                cur = 0
                for j in range(1, KCONV):
                    nxt = 1 - cur
                    t0 = p.op("vector", lambda E, o=conv_sc[nxt][:, :],
                              i=ucols[:, j:j + TC],
                              sc=convw_sb[:, et * KCONV + j: et * KCONV + j + 1],
                              a=conv_sc[cur][:, :]:
                              E.scalar_tensor_tensor(o, i, sc, a, OP.mult, OP.add),
                              waits=[t0])
                    cur = nxt
                txc = p.op("scalar", lambda E, o=xc_slab[:, mt * TC:(mt + 1) * TC],
                           i=conv_sc[cur][:, :], b=convb_sb[:, et:et + 1]:
                           E.activation(o, i, AF.Silu, bias=b),
                           waits=[t0] +
                           ([st_xc_prev] if st_xc_prev else []) +
                           ([xdbl_prev] if xdbl_prev else []))
                conv_toks.append(txc)
            conv_done_prev = conv_toks[-1]

            for si, (off, w) in enumerate(spans_t):
                for mt in range(MTS):
                    et = s * MTS + mt
                    st = (s == 0 and mt == 0)
                    sp = (s == NSLAB - 1 and mt == MTS - 1)
                    wts = [conv_toks[mt], wloads]
                    fn1 = (lambda E, ps=banks[4 + si],
                           lt=xproj_sb[:, et * 160: et * 160 + 128],
                           r=xc_slab[:, mt * TC + off: mt * TC + off + w],
                           st=st, sp=sp, w_=w:
                           E.matmul(ps[:, :w_], lt, r, start=st, stop=sp))
                    fn2 = (lambda E, ps=banks[6 + si],
                           lt=xproj_sb[:, et * 160 + 128: et * 160 + 160],
                           r=xc_slab[:, mt * TC + off: mt * TC + off + w],
                           st=st, sp=sp, w_=w:
                           E.matmul(ps[0:32, :w_], lt, r, start=st, stop=sp))
                    last_of_slab = (mt == MTS - 1 and si == len(spans_t) - 1)
                    if sp:
                        xdbl_final[si] = p.op("tensor", fn1, waits=wts)
                        tok2 = p.op("tensor", fn2, waits=wts)
                        xdbl_prev = tok2
                        if si == len(spans_t) - 1:
                            xdbl_final.append(tok2)
                    elif last_of_slab:
                        p.raw("tensor", fn1, waits=wts)
                        xdbl_prev = p.op("tensor", fn2, waits=wts)
                    else:
                        p.raw("tensor", fn1, waits=wts)
                        p.raw("tensor", fn2, waits=wts)

            st_xc_prev = p.dma(
                "sync", s_st,
                o_xc.rearrange("(et p) t -> p et t", p=128)[:, s * MTS:(s + 1) * MTS, :],
                xc_slab[:, :].rearrange("p (et t) -> p et t", et=MTS),
                waits=conv_toks)

        st_sg_prev = None
        for s in range(NSLAB):
            g_toks = []

            def evac_g2(m_idx, si, off, w, ps, mmtok, s=s, g_toks=g_toks):
                mt = m_idx - s * MTS
                extra = []
                if s == 0 and conv_done_prev is not None:
                    extra.append(conv_done_prev)
                if st_sg_prev is not None:
                    extra.append(st_sg_prev)
                tok = p.op("scalar",
                           lambda E, o=u_slab[:, mt * TC + off: mt * TC + off + w],
                           i=ps[:, :w]: E.activation(o, i, AF.Silu),
                           waits=[mmtok] + extra)
                g_toks.append(tok)
                return tok

            proj_stream(p, ws, winTg3, list(range(s * MTS, (s + 1) * MTS)), spans_t,
                        rhs_xn_tok, banks[:4], bstate, evac_g2, rhs_waits=[xn_done])
            st_sg_prev = p.dma(
                "sync", s_st,
                o_sg.rearrange("(et p) t -> p et t", p=128)[:, s * MTS:(s + 1) * MTS, :],
                u_slab[:, 0:MTS * TC].rearrange("p (et t) -> p et t", et=MTS),
                waits=g_toks)

        dtlow_toks = []
        bc_toks = []
        for si, (off, w) in enumerate(spans_t):
            t1 = p.op("scalar", lambda E, o=dtlow_sb[:, off:off + w],
                      i=banks[4 + si][:, :w]: E.activation(o, i, AF.Copy),
                      waits=[xdbl_final[si]])
            t2 = p.op("vector", lambda E, o=bc_sb[:, off:off + w],
                      i=banks[6 + si][0:32, :w]: E.tensor_copy(o, i),
                      waits=[xdbl_final[-1]])
            dtlow_toks.append(t1)
            bc_toks.append(t2)
        p.dma("sync", s_st, o_bc, bc_sb[:, :], waits=dtlow_toks + bc_toks)

        st_dt_prev = None
        prev_sp = None
        for me in range(ET):
            sp_toks = []
            for si, (off, w) in enumerate(spans_t):
                b = si
                waits = dtlow_toks + [wloads]
                if bstate.get(b) is not None:
                    waits.append(bstate[b])
                tok = p.op("tensor", lambda E, ps=banks[b],
                           lt=dtproj_sb[:, me * 128:(me + 1) * 128],
                           r=dtlow_sb[:, off:off + w], w_=w:
                           E.matmul(ps[:, :w_], lt, r, start=True, stop=True),
                           waits=waits)
                bcol = dtb_sb[:, me:me + 1]
                war = [prev_sp] if prev_sp else []
                ta = p.op("scalar", lambda E, o=sp_scA[:, :w], i=banks[b][:, :w], bc2=bcol:
                          E.activation(o, i, AF.Relu, bias=bc2), waits=[tok] + war)
                tb2 = p.op("scalar", lambda E, o=sp_scB[:, :w], i=banks[b][:, :w], bc2=bcol:
                           E.activation(o, i, AF.Abs, bias=bc2), waits=[tok] + war)
                tc2 = p.op("scalar", lambda E, o=sp_scB[:, :w], i=sp_scB[:, :w]:
                           E.activation(o, i, AF.Exp, scale=-1.0), waits=[tb2])
                td2 = p.op("scalar", lambda E, o=sp_scB[:, :w], i=sp_scB[:, :w]:
                           E.activation(o, i, AF.Ln, bias=1.0), waits=[tc2])
                te = p.op("vector", lambda E, o=dt_sc[:, off:off + w], a=sp_scA[:, :w],
                          bb=sp_scB[:, :w]: E.tensor_add(o, a, bb),
                          waits=[ta, td2] + ([st_dt_prev] if st_dt_prev else []))
                prev_sp = te
                sp_toks.append(te)
                bstate[b] = te
            st_dt_prev = p.dma("sync", s_st, o_dt[me * 128:(me + 1) * 128, :],
                               dt_sc[:, :], waits=sp_toks)

        xn_tok_view = xn_sb[:, :].rearrange("p (kt t) -> p kt t", kt=KT)[:, :, 3:TH]
        p.dma("sync", s_st, o_xn.rearrange("(kt p) t -> p kt t", p=128),
              xn_tok_view, waits=[xn_done])

        p.wait_only("gpsimd", (s_st, s_st.n))

        blk = stack.enter_context(nc.Block())
        p.run(blk)
    return nc


def build_launch_B():
    nc = bass.Bass("TRN2", target_bir_lowering=False)
    ETS = ES // 128
    TCH = globals().get("TCH", 2048)
    NCH_SEQ = SEQ // TCH

    xc_e = nc.dram_tensor("xc_e", [ES, TOKENS], BF16, kind="ExternalInput").ap()
    dt_e = nc.dram_tensor("dt_e", [ES, TOKENS], F32, kind="ExternalInput").ap()
    bcrow = nc.dram_tensor("bcrow", [32, TOKENS], F32, kind="ExternalInput").ap()
    alog = nc.dram_tensor("alog", [128, ETS * NST], F32, kind="ExternalInput").ap()
    dvec = nc.dram_tensor("dvec", [128, ETS], F32, kind="ExternalInput").ap()
    o_y = nc.dram_tensor("o_y", [ES, TOKENS], BF16, kind="ExternalOutput").ap()

    with ExitStack() as stack:
        p = Prog(nc, stack)
        sb = lambda n, s, d: stack.enter_context(nc.sbuf_tensor(n, s, d))
        s_in = p.sem("s_in")
        s_c = p.sem("s_c")
        s_st = p.sem("s_st")
        p.init_consts([1e-5, 1.0, -1.0], s_c)

        dt_sb = sb("dt_sb", [128, ETS * TCH], F32)
        xc_sb = sb("xc_sb", [128, ETS * TCH], BF16)
        dtu_sb = sb("dtu_sb", [128, ETS * TCH], BF16)
        bbc_sb = [sb(f"bbc{i}", [128, TCH], BF16) for i in range(2)]
        cbc_sb = [sb(f"cbc{i}", [128, TCH], BF16) for i in range(2)]
        dA_sb = [sb(f"dA{i}", [128, TCH], F32) for i in range(2)]
        X_sb = [sb(f"X{i}", [128, TCH], BF16) for i in range(2)]
        s_sb = [sb(f"s{i}", [128, TCH], BF16) for i in range(2)]
        sC_sb = [sb(f"sC{i}", [128, TCH], BF16) for i in range(2)]
        y_sb = [sb(f"y{i}", [128, TCH], BF16) for i in range(2)]
        slast_sb = sb("slast", [128, ETS * NST], BF16)
        A_sb = sb("A_sb", [128, ETS * NST], F32)
        Dv_sb = sb("Dv_sb", [128, ETS], F32)
        ident = sb("ident", [128, 128], BF16)
        banks = [stack.enter_context(nc.psum_tensor(f"pb{i}", [128, 512], F32))
                 for i in range(8)]

        idc = nc.inline_tensor(np.eye(128, dtype=BF), name="ident_c")
        p.dma("sync", s_in, ident[:, :], idc.ap())
        p.dma("sync", s_in, A_sb[:, :], alog)
        p.dma("sync", s_in, Dv_sb[:, :], dvec)
        loads = (s_in, s_in.n)
        for e in ("tensor", "vector", "scalar", "gpsimd"):
            p.wait_only(e, loads)

        ta = p.op("scalar", lambda E: E.activation(A_sb[:, :], A_sb[:, :], AF.Exp),
                  waits=[loads])
        ta = p.op("vector", lambda E: E.tensor_scalar_mul(A_sb[:, :], A_sb[:, :], -1.0),
                  waits=[ta])

        s_dt = p.sem("s_dt")
        s_xc = p.sem("s_xc")
        s_bc = [p.sem(f"s_bc{i}") for i in range(2)]
        s_cc = [p.sem(f"s_cc{i}") for i in range(2)]
        s_y = [p.sem(f"s_y{i}") for i in range(2)]

        dt3 = dt_e.rearrange("(et p) t -> p et t", p=128)
        xc3 = xc_e.rearrange("(et p) t -> p et t", p=128)
        oy3 = o_y.rearrange("(et p) t -> p et t", p=128)

        spans = _spans(TCH)
        chunk_cons = None
        bc_cons = [None, None]
        buf_cons = {}
        bank_rel = {}
        prev_scan = None
        st_prev = {}

        ci = 0
        for b in range(BATCH):
            for h in range(NCH_SEQ):
                c0 = b * SEQ + h * TCH
                wdt = list(chunk_cons) if chunk_cons else []
                tdt = p.dma("sync", s_dt, dt_sb[:, :].rearrange("p (et t) -> p et t", et=ETS),
                            dt3[:, :, c0:c0 + TCH], waits=wdt)
                txc = p.dma("sync", s_xc, xc_sb[:, :].rearrange("p (et t) -> p et t", et=ETS),
                            xc3[:, :, c0:c0 + TCH], waits=wdt)
                dtu_toks = []
                for et in range(ETS):
                    tok = p.op("vector", lambda E, o=dtu_sb[:, et * TCH:(et + 1) * TCH],
                               a=dt_sb[:, et * TCH:(et + 1) * TCH],
                               bq=xc_sb[:, et * TCH:(et + 1) * TCH]:
                               E.tensor_tensor(o, a, bq, OP.mult),
                               waits=[tdt, txc] + (chunk_cons if chunk_cons else []))
                    dtu_toks.append(tok)
                y_evac_toks = []
                for et in range(ETS):
                    bset = [banks[(et % 2) * 4 + i] for i in range(4)]
                    mm_tok = None
                    for n in range(NST):
                        pb = (et * NST + n) % 2
                        an = et * NST + n
                        wb = [bc_cons[pb]] if bc_cons[pb] else []
                        tb = p.dma("gpsimd", s_bc[pb], bbc_sb[pb][:, :],
                                   bcrow[n:n + 1, c0:c0 + TCH].broadcast_to([128, TCH]),
                                   waits=wb)
                        tc = p.dma("gpsimd", s_cc[pb], cbc_sb[pb][:, :],
                                   bcrow[16 + n:17 + n, c0:c0 + TCH].broadcast_to([128, TCH]),
                                   waits=wb)
                        wda = [tdt, ta, buf_cons.get(("dA", pb))]
                        tda = p.op("scalar", lambda E, o=dA_sb[pb][:, :],
                                   i=dt_sb[:, et * TCH:(et + 1) * TCH],
                                   sc=A_sb[:, an:an + 1]:
                                   E.activation(o, i, AF.Exp, scale=sc),
                                   waits=[t for t in wda if t])
                        wx = [dtu_toks[et], tb, buf_cons.get(("X", pb))]
                        tx = p.op("vector", lambda E, o=X_sb[pb][:, :],
                                  a=dtu_sb[:, et * TCH:(et + 1) * TCH], bq=bbc_sb[pb][:, :]:
                                  E.tensor_tensor(o, a, bq, OP.mult),
                                  waits=[t for t in wx if t])
                        init = 0.0 if h == 0 else slast_sb[:, an:an + 1]
                        wsc = [tda, tx, buf_cons.get(("s", pb))]
                        if h == 1:
                            wsc.append(buf_cons.get(("slast", an)))
                        tsc = p.op("vector", lambda E, o=s_sb[pb][:, :], da=dA_sb[pb][:, :],
                                   xx=X_sb[pb][:, :], ini=init:
                                   E.tensor_tensor_scan(o, da, xx, ini, OP.mult, OP.add),
                                   waits=[t for t in wsc if t])
                        buf_cons[("dA", pb)] = tsc
                        buf_cons[("X", pb)] = tsc
                        last_tda = tda
                        if h == 1:
                            buf_cons[("slastR", an)] = tsc
                        if h == 0:
                            tsl = p.op("vector", lambda E, o=slast_sb[:, an:an + 1],
                                       i=s_sb[pb][:, TCH - 1:TCH]:
                                       E.tensor_copy(o, i),
                                       waits=[tsc] +
                                       ([buf_cons.get(("slastR", an))] if
                                        buf_cons.get(("slastR", an)) else []))
                            buf_cons[("slast", an)] = tsl
                        wsb = [tsc, tc, buf_cons.get(("sC", pb))]
                        tsb = p.op("vector", lambda E, o=sC_sb[pb][:, :], a=s_sb[pb][:, :],
                                   bq=cbc_sb[pb][:, :]:
                                   E.tensor_tensor(o, a, bq, OP.mult),
                                   waits=[t for t in wsb if t])
                        buf_cons[("s", pb)] = tsb
                        for si, (off, w) in enumerate(spans):
                            wmm = [tsb]
                            if n == 0 and bank_rel.get((et % 2, si)) is not None:
                                wmm.append(bank_rel[(et % 2, si)])
                            fn = (lambda E, ps=bset[si], r=sC_sb[pb][:, off:off + w],
                                  st=(n == 0), sp=(n == NST - 1), w_=w:
                                  E.matmul(ps[:, :w_], ident[:, :], r, start=st, stop=sp))
                            if si == len(spans) - 1:
                                mm_tok = p.op("tensor", fn, waits=wmm)
                            else:
                                p.raw("tensor", fn, waits=wmm)
                        bc_cons[pb] = mm_tok if mm_tok else tsb
                    ye = None
                    for si, (off, w) in enumerate(spans):
                        ye = p.op("vector", lambda E, o=y_sb[et % 2][:, off:off + w],
                                  i=xc_sb[:, et * TCH + off: et * TCH + off + w],
                                  sc=Dv_sb[:, et:et + 1], ps=bset[si], w_=w:
                                  E.scalar_tensor_tensor(o, i, sc, ps[:, :w_],
                                                         OP.mult, OP.add),
                                  waits=[mm_tok, txc] +
                                  ([st_prev.get(et % 2)] if st_prev.get(et % 2) else []))
                        bank_rel[(et % 2, si)] = ye
                        y_evac_toks.append(ye)
                    st_prev[et % 2] = p.dma("sync", s_y[et % 2], oy3[:, et, c0:c0 + TCH],
                                            y_sb[et % 2][:, :],
                                            waits=[ye])
                chunk_cons = [y_evac_toks[-1], last_tda]
                ci += 1

        p.wait_only("gpsimd", (s_y[0], s_y[0].n))
        p.wait_only("gpsimd", (s_y[1], s_y[1].n))
        blk = stack.enter_context(nc.Block())
        p.run(blk)
    return nc


def moe_expert_block(p, ws, hbuf, u_sc, cbc_ap, xn_rhs, spans, banks, bstate,
                     wgT3, wuT3, wdT3, i_mt, d_kt, i_kt, d_mt, macc, macc_init,
                     xn_ready, cbc_tok, u_cons, macc_guard, b1col=None, b2term=None):
    TCW = TC
    g_toks = {}
    for mt in range(i_mt):
        src = (wgT3 if wgT3 is not None else wuT3)
        bufi, wtok = ws.load(src[:, :, mt * 128:(mt + 1) * 128], d_kt)
        mm_last = None
        for si, (off, w) in enumerate(spans):
            b = (mt * len(spans) + si) % 4
            waits = [wtok, xn_ready]
            if bstate.get(b) is not None:
                waits.append(bstate[b])
            for kt in range(d_kt):
                fn = (lambda E, ps=banks[b], wt=ws.tile(bufi, kt), r=xn_rhs(kt, off, w),
                      st=(kt == 0), sp=(kt == d_kt - 1), w_=w:
                      E.matmul(ps[:, :w_], wt, r, start=st, stop=sp))
                if kt == d_kt - 1:
                    mm_last = p.op("tensor", fn, waits=waits)
                else:
                    p.raw("tensor", fn, waits=waits)
                waits = []
            tok = p.op("scalar", lambda E, o=hbuf[:, mt * TCW + off: mt * TCW + off + w],
                       i=banks[b][:, :w], bc=b1col(mt) if b1col else 0.0:
                       E.activation(o, i, AF.Silu, bias=bc),
                       waits=[mm_last] + ([g_toks[(mt, si)]] if (mt, si) in g_toks else [])
                       + ([macc_guard] if macc_guard else []))
            g_toks[(mt, si)] = tok
            bstate[b] = tok
        ws.consumed(bufi, mm_last)
    h_toks = {}
    if wgT3 is not None:
        for mt in range(i_mt):
            bufi, wtok = ws.load(wuT3[:, :, mt * 128:(mt + 1) * 128], d_kt)
            mm_last = None
            for si, (off, w) in enumerate(spans):
                b = (mt * len(spans) + si) % 4
                waits = [wtok, xn_ready]
                if bstate.get(b) is not None:
                    waits.append(bstate[b])
                for kt in range(d_kt):
                    fn = (lambda E, ps=banks[b], wt=ws.tile(bufi, kt), r=xn_rhs(kt, off, w),
                          st=(kt == 0), sp=(kt == d_kt - 1), w_=w:
                          E.matmul(ps[:, :w_], wt, r, start=st, stop=sp))
                    if kt == d_kt - 1:
                        mm_last = p.op("tensor", fn, waits=waits)
                    else:
                        p.raw("tensor", fn, waits=waits)
                    waits = []
                ue = p.op("scalar", lambda E, o=u_sc[si][:, :w], i=banks[b][:, :w]:
                          E.activation(o, i, AF.Copy),
                          waits=[mm_last] + ([u_cons[si]] if u_cons[si] else []))
                th = p.op("vector", lambda E, o=hbuf[:, mt * TCW + off: mt * TCW + off + w],
                          a=hbuf[:, mt * TCW + off: mt * TCW + off + w], bq=u_sc[si][:, :w]:
                          E.tensor_tensor(o, a, bq, OP.mult),
                          waits=[ue, g_toks[(mt, si)]])
                u_cons[si] = th
                if cbc_ap is not None:
                    th = p.op("vector", lambda E, o=hbuf[:, mt * TCW + off: mt * TCW + off + w],
                              a=hbuf[:, mt * TCW + off: mt * TCW + off + w],
                              bq=cbc_ap[:, off:off + w]:
                              E.tensor_tensor(o, a, bq, OP.mult),
                              waits=[th, cbc_tok] if cbc_tok else [th])
                h_toks[(mt, si)] = th
                bstate[b] = th
            ws.consumed(bufi, mm_last)
    else:
        for mt in range(i_mt):
            for si, (off, w) in enumerate(spans):
                th = g_toks[(mt, si)]
                if cbc_ap is not None:
                    th = p.op("vector", lambda E, o=hbuf[:, mt * TCW + off: mt * TCW + off + w],
                              a=hbuf[:, mt * TCW + off: mt * TCW + off + w],
                              bq=cbc_ap[:, off:off + w]:
                              E.tensor_tensor(o, a, bq, OP.mult),
                              waits=[th, cbc_tok] if cbc_tok else [th])
                h_toks[(mt, si)] = th

    h_ready = [h_toks[(mt, si)] for mt in range(i_mt) for si in range(len(spans))]
    macc_toks = []
    for mtd in range(d_mt):
        bufi, wtok = ws.load(wdT3[:, :, mtd * 128:(mtd + 1) * 128], i_kt)
        mm_last = None
        for si, (off, w) in enumerate(spans):
            b = 4 + ((mtd * len(spans) + si) % 4)
            waits = [wtok] + h_ready
            if bstate.get(b) is not None:
                waits.append(bstate[b])
            for kt in range(i_kt):
                fn = (lambda E, ps=banks[b], wt=ws.tile(bufi, kt),
                      r=hbuf[:, kt * TCW + off: kt * TCW + off + w],
                      st=(kt == 0), sp=(kt == i_kt - 1), w_=w:
                      E.matmul(ps[:, :w_], wt, r, start=st, stop=sp))
                if kt == i_kt - 1:
                    mm_last = p.op("tensor", fn, waits=waits)
                else:
                    p.raw("tensor", fn, waits=waits)
                waits = []
            mv = macc[:, mtd * TCW + off: mtd * TCW + off + w]
            if macc_init:
                tok = p.op("vector", lambda E, o=mv, i=banks[b][:, :w]:
                           E.tensor_copy(o, i),
                           waits=[mm_last] + ([macc_guard] if macc_guard else []))
            else:
                tok = p.op("vector", lambda E, o=mv, i=banks[b][:, :w]:
                           E.tensor_tensor(o, o, i, OP.add), waits=[mm_last])
            macc_toks.append(tok)
            bstate[b] = tok
        ws.consumed(bufi, mm_last)
    return macc_toks, h_ready[-1]


def build_launch_C():
    nc = bass.Bass("TRN2", target_bir_lowering=False)
    KT = D // 128
    ET = E // 128
    IT = IEXP // 128
    SHT = ISH // 128
    SHH = SHT // 2
    y_t = nc.dram_tensor("y_t", [E, TC], BF16, kind="ExternalInput").ap()
    sg = nc.dram_tensor("sg", [E, TC], BF16, kind="ExternalInput").ap()
    xn0 = nc.dram_tensor("xn0", [D, TC], BF16, kind="ExternalInput").ap()
    x0 = nc.dram_tensor("x0", [D, TC], F32, kind="ExternalInput").ap()
    woutT = nc.dram_tensor("woutT", [E, D], F32, kind="ExternalInput").ap()
    gatewT = nc.dram_tensor("gatewT", [D, NEXP], F32, kind="ExternalInput").ap()
    shgT = nc.dram_tensor("shgT", [D, ISH], F32, kind="ExternalInput").ap()
    shuT = nc.dram_tensor("shuT", [D, ISH], F32, kind="ExternalInput").ap()
    shdT = nc.dram_tensor("shdT", [ISH, D], F32, kind="ExternalInput").ap()
    egT = nc.dram_tensor("egT", [NEXP, D, IEXP], F32, kind="ExternalInput").ap()
    euT = nc.dram_tensor("euT", [NEXP, D, IEXP], F32, kind="ExternalInput").ap()
    edT = nc.dram_tensor("edT", [NEXP, IEXP, D], F32, kind="ExternalInput").ap()
    o_x1 = nc.dram_tensor("o_x1", [D, TC], F32, kind="ExternalOutput").ap()
    o_xn1 = nc.dram_tensor("o_xn1", [D, TC], BF16, kind="ExternalOutput").ap()

    spans = _spans(TC)
    nsp = len(spans)

    with ExitStack() as stack:
        p = Prog(nc, stack)
        sb = lambda n, s, d: stack.enter_context(nc.sbuf_tensor(n, s, d))
        s_in = p.sem("s_in")
        s_c = p.sem("s_c")
        p.init_consts([1e-5, 1.0, -1.0], s_c)
        s0_sb = sb("s0_sb", [128, KT * TC], BF16)
        macc = sb("macc", [128, KT * TC], F32)
        ct_sb = sb("ct_sb", [1, NEXP * TC], BF16)
        gatew_sb = sb("gatew_sb", [128, KT * NEXP], BF16)
        ones_col = sb("ones_col", [128, 1], BF16)
        ones_row32 = sb("ones_row32", [1, 128], F32)
        ones_rowbf = sb("ones_rowbf", [1, 128], BF16)
        ident32 = sb("ident32", [128, 128], F32)
        f32_sc = sb("f32_sc", [128, 512], F32)
        banks = [stack.enter_context(nc.psum_tensor(f"pb{i}", [128, 512], F32))
                 for i in range(8)]
        ws = WStream(p, "wc", E)

        co = nc.inline_tensor(np.ones((128, 1), dtype=BF), name="ones_col_c")
        cr = nc.inline_tensor(np.ones((1, 128), dtype=np.float32), name="ones_row_c")
        crb = nc.inline_tensor(np.ones((1, 128), dtype=BF), name="ones_rowbf_c")
        idc32 = nc.inline_tensor(np.eye(128, dtype=np.float32), name="id32_c")
        p.dma("sync", s_in, ones_col[:, :], co.ap())
        p.dma("sync", s_in, ones_row32[:, :], cr.ap())
        p.dma("sync", s_in, ones_rowbf[:, :], crb.ap())
        p.dma("sync", s_in, ident32[:, :], idc32.ap())
        s_gw = p.sem("s_gw")
        p.dma("gpsimd", s_gw, gatew_sb[:, :].rearrange("p (kt e) -> p kt e", kt=KT),
              _kt_view(gatewT))
        loads = (s_in, s_in.n)
        gw_tok = (s_gw, s_gw.n)
        for e in ("tensor", "vector", "scalar", "gpsimd"):
            p.wait_only(e, loads)

        with nc.sbuf_tensor("z_sb", [128, ET * TC], BF16) as z_sb, \
             nc.sbuf_tensor("y_sc", [128, 2 * TC], BF16) as y_sc, \
             nc.sbuf_tensor("sg_sc", [128, 2 * TC], BF16) as sg_sc:
            s_y = [p.sem(f"s_yl{i}") for i in range(2)]
            s_g2 = [p.sem(f"s_gl{i}") for i in range(2)]
            y3 = y_t.rearrange("(et p) t -> p et t", p=128)
            sg3 = sg.rearrange("(et p) t -> p et t", p=128)
            zc = {}
            for et in range(ET):
                pb = et % 2
                wz = [zc[pb]] if pb in zc else []
                ty = p.dma("sync", s_y[pb], y_sc[:, pb * TC:(pb + 1) * TC],
                           y3[:, et, :], waits=wz)
                tg = p.dma("sync", s_g2[pb], sg_sc[:, pb * TC:(pb + 1) * TC],
                           sg3[:, et, :], waits=wz)
                tz = p.op("vector", lambda E, o=z_sb[:, et * TC:(et + 1) * TC],
                          a=y_sc[:, pb * TC:(pb + 1) * TC],
                          bq=sg_sc[:, pb * TC:(pb + 1) * TC]:
                          E.tensor_tensor(o, a, bq, OP.mult), waits=[ty, tg])
                zc[pb] = tz
            z_done = zc[(ET - 1) % 2]

            def rhs_z(kt, off, w):
                return z_sb[:, kt * TC + off: kt * TC + off + w]

            bstate = {}

            def evac_s0(m_idx, si, off, w, ps, mmtok):
                return p.op("scalar",
                            lambda E, o=s0_sb[:, m_idx * TC + off: m_idx * TC + off + w],
                            i=ps[:, :w]: E.activation(o, i, AF.Copy), waits=[mmtok])

            proj_stream(p, ws, _kt_view(woutT), list(range(KT)), spans, rhs_z,
                        banks[:4], bstate, evac_s0, rhs_waits=[z_done, zc[ET % 2]])
            ph1 = p.tokens()

        with nc.sbuf_tensor("xn0_sb", [128, KT * TC], BF16) as xn0_sb, \
             nc.sbuf_tensor("hbuf", [128, IT * TC], BF16) as hbuf, \
             nc.sbuf_tensor("u_sc0", [128, 512], BF16) as u_sc0, \
             nc.sbuf_tensor("u_sc1", [128, 512], BF16) as u_sc1, \
             nc.sbuf_tensor("cbc", [128, TC], F32) as cbc, \
             nc.sbuf_tensor("r_sc", [128, 11 * NEXP], F32) as r_sc:
            p.barrier(ph1)
            s_xn = p.sem("s_xn")
            txn = p.dma("sync", s_xn, xn0_sb[:, :].rearrange("p (kt t) -> p kt t", kt=KT),
                        _kt_view(xn0), waits=ph1)
            xn_tok = (s_xn, s_xn.n)
            u_sc = [u_sc0, u_sc1]

            def rhs_xn(kt, off, w):
                return xn0_sb[:, kt * TC + off: kt * TC + off + w]

            NTCH = TC // 128
            ct_done = None
            bstate = {}
            prev_r = None
            for tch in range(NTCH):
                b = 6 + (tch % 2)
                mmw = [xn_tok, gw_tok] + ph1
                if bstate.get(b) is not None:
                    mmw.append(bstate[b])
                mml = None
                for kt in range(KT):
                    fn = (lambda E, ps=banks[b],
                          lt=xn0_sb[:, kt * TC + tch * 128: kt * TC + (tch + 1) * 128],
                          r=gatew_sb[:, kt * NEXP:(kt + 1) * NEXP],
                          st=(kt == 0), sp=(kt == KT - 1):
                          E.matmul(ps[:, :NEXP], lt, r, start=st, stop=sp))
                    if kt == KT - 1:
                        mml = p.op("tensor", fn, waits=mmw)
                    else:
                        p.raw("tensor", fn, waits=mmw)
                    mmw = []
                lg = r_sc[:, 0:NEXP]
                nm = r_sc[:, NEXP:NEXP + 1]
                ex = r_sc[:, 2 * NEXP:3 * NEXP]
                sm = r_sc[:, 3 * NEXP:3 * NEXP + 1]
                pr = r_sc[:, 4 * NEXP:5 * NEXP]
                m1 = r_sc[:, 5 * NEXP:5 * NEXP + 1]
                k1 = r_sc[:, 6 * NEXP:7 * NEXP]
                p2 = r_sc[:, 7 * NEXP:8 * NEXP]
                m2 = r_sc[:, 8 * NEXP:8 * NEXP + 1]
                k2 = r_sc[:, 9 * NEXP:10 * NEXP]
                wv = r_sc[:, 10 * NEXP:10 * NEXP + 1]
                cc = r_sc[:, 10 * NEXP + 1:11 * NEXP + 1] if False else r_sc[:, NEXP + 1:2 * NEXP + 1]
                war = [prev_r] if prev_r else []
                t0 = p.op("vector", lambda E, o=lg, i=banks[b][:, :NEXP]:
                          E.tensor_copy(o, i), waits=[mml] + war)
                bstate[b] = t0
                t1 = p.op("vector", lambda E: E.tensor_reduce(nm, lg, AX.X, OP.max, negate=True),
                          waits=[t0])
                t2 = p.op("scalar", lambda E: E.activation(ex, lg, AF.Exp, bias=nm),
                          waits=[t1] + war)
                t3 = p.op("vector", lambda E: E.reduce_sum(sm, ex, AX.X), waits=[t2])
                t4 = p.op("vector", lambda E: E.reciprocal(sm, sm), waits=[t3])
                t5 = p.op("vector", lambda E: E.tensor_scalar(pr, ex, sm, None, OP.mult),
                          waits=[t4])
                t6 = p.op("vector", lambda E: E.tensor_reduce(m1, pr, AX.X, OP.max),
                          waits=[t5])
                t7 = p.op("vector", lambda E: E.tensor_single_scalar(k1, pr, m1, OP.is_equal),
                          waits=[t6])
                t8 = p.op("vector", lambda E: E.scalar_tensor_tensor(p2, k1, -1e9, pr, OP.mult, OP.add),
                          waits=[t7])
                t9 = p.op("vector", lambda E: E.tensor_reduce(m2, p2, AX.X, OP.max),
                          waits=[t8])
                t10 = p.op("vector", lambda E: E.tensor_single_scalar(k2, p2, m2, OP.is_equal),
                           waits=[t9])
                t11 = p.op("vector", lambda E: E.tensor_add(wv, m1, m2), waits=[t10])
                t12 = p.op("vector", lambda E: E.tensor_scalar(wv, wv, 1e-20, None, OP.add),
                           waits=[t11])
                t13 = p.op("vector", lambda E: E.reciprocal(wv, wv), waits=[t12])
                t14 = p.op("vector", lambda E: E.tensor_add(cc, k1, k2), waits=[t13])
                t15 = p.op("vector", lambda E: E.tensor_mul(cc, cc, pr), waits=[t14])
                t16 = p.op("vector", lambda E: E.tensor_scalar(cc, cc, wv, None, OP.mult),
                           waits=[t15])
                prev_r = t16
                for ex_i in range(NEXP):
                    b2 = 4 + (ex_i % 2)
                    mw = [t16] + ([bstate.get(b2)] if bstate.get(b2) else [])
                    tt = p.op("tensor", lambda E, ps=banks[b2], col=cc[:, ex_i:ex_i + 1]:
                              E.matmul(ps[0:1, 0:128], col, ident32[:, :],
                                       start=True, stop=True), waits=mw)
                    te = p.op("vector", lambda E,
                              o=ct_sb[0:1, ex_i * TC + tch * 128: ex_i * TC + (tch + 1) * 128],
                              i=banks[b2][0:1, 0:128]: E.tensor_copy(o, i), waits=[tt])
                    bstate[b2] = te
                    ct_done = te
                    prev_r = te

            u_cons = [None] * nsp
            bstate = {}
            shg3 = _kt_view(shgT)
            shu3 = _kt_view(shuT)
            shd3 = _kt_view(shdT)
            for half in range(2):
                macc_toks, _ = moe_expert_block(
                    p, ws, hbuf, u_sc, None, rhs_xn, spans, banks, bstate,
                    shg3[:, :, half * IEXP:(half + 1) * IEXP],
                    shu3[:, :, half * IEXP:(half + 1) * IEXP],
                    shd3[:, half * IT:(half + 1) * IT, :],
                    IT, KT, IT, KT, macc, half == 0, xn_tok, None, u_cons,
                    macc_guard=None)
            cbc_tok = None
            for ex_i in range(NEXP):
                cb_toks = []
                for si, (off, w) in enumerate(spans):
                    b = 6 + (si % 2)
                    mmw = [ct_done] + ([bstate.get(b)] if bstate.get(b) else [])
                    tt = p.op("tensor", lambda E, ps=banks[b],
                              r=ct_sb[0:1, ex_i * TC + off: ex_i * TC + off + w],
                              w_=w: E.matmul(ps[:, :w_], ones_rowbf[:, :], r,
                                             start=True, stop=True),
                              waits=mmw)
                    tev = p.op("scalar", lambda E, o=cbc[:, off:off + w], i=banks[b][:, :w]:
                               E.activation(o, i, AF.Copy),
                               waits=[tt] + ([cbc_tok] if cbc_tok else []))
                    bstate[b] = tev
                    cb_toks.append(tev)
                macc_toks, cbc_tok = moe_expert_block(
                    p, ws, hbuf, u_sc, cbc[:, :], rhs_xn, spans, banks, bstate,
                    _kt_view(egT[ex_i]), _kt_view(euT[ex_i]), _kt_view(edT[ex_i]),
                    IT, KT, IT, KT, macc, False, xn_tok,
                    cb_toks[-1], u_cons, macc_guard=None)
            g0_toks = []
            gprev = None
            for kt in range(KT):
                gprev = p.op("scalar", lambda E, o=macc[:, kt * TC:(kt + 1) * TC],
                             i=macc[:, kt * TC:(kt + 1) * TC]:
                             E.activation(o, i, AF.Silu),
                             waits=macc_toks + ([gprev] if gprev else []))
                g0_toks.append(gprev)
            ph2 = p.tokens()

        with nc.sbuf_tensor("x1b", [128, KT * TC], BF16) as x1b, \
             nc.sbuf_tensor("sq_sc", [128, TC], BF16) as sq_sc, \
             nc.sbuf_tensor("rows", [1, 6 * TC], F32) as rows, \
             nc.sbuf_tensor("x0_sc", [128, 2 * TC], F32) as x0_sc, \
             nc.sbuf_tensor("x1_sc", [128, 2 * TC], F32) as x1_sc:
            p.barrier(ph2)
            s_x0 = [p.sem(f"s_x0{i}") for i in range(2)]
            s_x1 = [p.sem(f"s_x1{i}") for i in range(2)]
            s_st = p.sem("s_st")
            consts = {"ones_col": ones_col[:, :], "ones_row32": ones_row32[:, :],
                      "f32_sc": f32_sc}
            prevx = {}
            x1_done = []
            for kt in range(KT):
                pb = kt % 2
                wx = prevx.get(pb, [])
                tx0 = p.dma("sync", s_x0[pb], x0_sc[:, pb * TC:(pb + 1) * TC],
                            _kt_view(x0)[:, kt, :], waits=wx)
                tm = p.op("vector", lambda E, o=x1_sc[:, pb * TC:(pb + 1) * TC],
                          a=s0_sb[:, kt * TC:(kt + 1) * TC],
                          bq=macc[:, kt * TC:(kt + 1) * TC]:
                          E.tensor_tensor(o, a, bq, OP.mult),
                          waits=[g0_toks[kt]] + wx)
                ta = p.op("vector", lambda E, o=x1_sc[:, pb * TC:(pb + 1) * TC],
                          a=x1_sc[:, pb * TC:(pb + 1) * TC],
                          bq=x0_sc[:, pb * TC:(pb + 1) * TC]:
                          E.tensor_tensor(o, a, bq, OP.add), waits=[tm, tx0])
                tst = p.dma("sync", s_x1[pb], _kt_view(o_x1)[:, kt, :],
                            x1_sc[:, pb * TC:(pb + 1) * TC], waits=[ta])
                tb = p.op("scalar", lambda E, o=x1b[:, kt * TC:(kt + 1) * TC],
                          i=x1_sc[:, pb * TC:(pb + 1) * TC]:
                          E.activation(o, i, AF.Copy), waits=[ta])
                prevx[pb] = [tst, tb]
                x1_done.append(tb)
            xn1_done = layernorm_fm(p, x1b, x1b, sq_sc, rows, banks, consts, TC, KT,
                                    x_ready=x1_done[-1], bc_banks=(6, 7))
            p.dma("sync", s_st, o_xn1.rearrange("(kt p) t -> p kt t", p=128),
                  x1b[:, :].rearrange("p (kt t) -> p kt t", kt=KT),
                  waits=[xn1_done])
            p.wait_only("gpsimd", (s_st, s_st.n))
            p.wait_only("gpsimd", (s_x1[0], s_x1[0].n))
            p.wait_only("gpsimd", (s_x1[1], s_x1[1].n))

        blk = stack.enter_context(nc.Block())
        p.run(blk)
    return nc


def build_launch_D():
    nc = bass.Bass("TRN2", target_bir_lowering=False)
    NH = 4
    KTT = SEQ // 128
    NQC = SEQ // 512
    KT = D // 128
    SCALE = 1.0 / float(np.sqrt(HDIM))

    xn1b = nc.dram_tensor("xn1b", [D, SEQ], BF16, kind="ExternalInput").ap()
    wqkvT = nc.dram_tensor("wqkvT", [128, 12 * (D // 128) * 128], F32,
                           kind="ExternalInput").ap()
    bqkv = nc.dram_tensor("bqkv", [128, 12], F32, kind="ExternalInput").ap()
    alibi = nc.dram_tensor("alibi", [NH, SEQ], F32, kind="ExternalInput").ap()
    alibi_colT = nc.dram_tensor("alibi_colT", [128, NH * KTT], F32,
                                kind="ExternalInput").ap()
    o_ctx = nc.dram_tensor("o_ctx", [NH * HDIM, SEQ], BF16, kind="ExternalOutput").ap()

    with ExitStack() as stack:
        p = Prog(nc, stack)
        sb = lambda n, s, d: stack.enter_context(nc.sbuf_tensor(n, s, d))
        s_in = p.sem("s_in")
        s_c = p.sem("s_c")
        p.init_consts([1e-5, 1.0, -1.0, SCALE], s_c)
        q_sb = sb("q_sb", [128, NH * SEQ], BF16)
        k_sb = sb("k_sb", [128, NH * SEQ], BF16)
        v_sb = sb("v_sb", [128, KTT * NH * 128], BF16)
        acolT = sb("acolT", [128, NH * KTT], F32)
        ident = sb("ident", [128, 128], BF16)
        ident32 = sb("ident32", [128, 128], F32)
        ones_row32 = sb("ones_row32", [1, 128], F32)
        banks = [stack.enter_context(nc.psum_tensor(f"pb{i}", [128, 512], F32))
                 for i in range(6)]
        banksb = [stack.enter_context(nc.psum_tensor(f"pt{i}", [128, 128], BF16))
                  for i in range(2)]

        idc = nc.inline_tensor(np.eye(128, dtype=BF), name="ident_c")
        idc32 = nc.inline_tensor(np.eye(128, dtype=np.float32), name="id32_c")
        cr = nc.inline_tensor(np.ones((1, 128), dtype=np.float32), name="ones_row_c")
        p.dma("sync", s_in, ident[:, :], idc.ap())
        p.dma("sync", s_in, ident32[:, :], idc32.ap())
        p.dma("sync", s_in, ones_row32[:, :], cr.ap())
        p.dma("sync", s_in, acolT[:, :], alibi_colT)
        loads = (s_in, s_in.n)
        for e in ("tensor", "vector", "scalar", "gpsimd"):
            p.wait_only(e, loads)

        with nc.sbuf_tensor("xn_sc", [128, 2 * KT * 512], BF16) as xn_sc, \
             nc.sbuf_tensor("wq_sb", [128, 12 * KT * 128], BF16) as wq_sb, \
             nc.sbuf_tensor("bq_sb", [128, 12], F32) as bq_sb, \
             nc.sbuf_tensor("v_sc", [128, 512], BF16) as v_sc:
            s_w = p.sem("s_w")
            s_b = p.sem("s_b")
            tw = p.dma("gpsimd", s_w, wq_sb[:, :], wqkvT)
            tb = p.dma("sync", s_b, bq_sb[:, :], bqkv)
            s_xn = [p.sem(f"s_xn{i}") for i in range(2)]
            xn3 = _kt_view(xn1b)
            xn_cons = {}
            vp_toks = {}
            bstate = {}
            vtrans_prev = {}
            for nch in range(NQC):
                pb = nch % 2
                wx = xn_cons.get(pb, [])
                txn = p.dma("sync", s_xn[pb],
                            xn_sc[:, pb * KT * 512:(pb + 1) * KT * 512]
                            .rearrange("p (kt t) -> p kt t", kt=KT),
                            xn3[:, :, nch * 512:(nch + 1) * 512], waits=wx)
                last_mm = None
                for m in range(12):
                    b = (nch * 12 + m) % 4
                    mmw = [txn, tw]
                    if bstate.get(b) is not None:
                        mmw.append(bstate[b])
                    mml = None
                    for kt in range(KT):
                        fn = (lambda E, ps=banks[b],
                              wt=wq_sb[:, (m * KT + kt) * 128:(m * KT + kt + 1) * 128],
                              r=xn_sc[:, pb * KT * 512 + kt * 512:(pb * KT + kt) * 512 + 512],
                              st=(kt == 0), sp=(kt == KT - 1):
                              E.matmul(ps[:, :], wt, r, start=st, stop=sp))
                        if kt == KT - 1:
                            mml = p.op("tensor", fn, waits=mmw)
                        else:
                            p.raw("tensor", fn, waits=mmw)
                        mmw = []
                    last_mm = mml
                    if m < 8:
                        dst = q_sb if m < 4 else k_sb
                        h = m % 4
                        tok = p.op("scalar", lambda E,
                                   o=dst[:, h * SEQ + nch * 512: h * SEQ + nch * 512 + 512],
                                   i=banks[b][:, :], bc=bq_sb[:, m:m + 1]:
                                   E.activation(o, i, AF.Identity, bias=bc),
                                   waits=[mml, tb])
                        bstate[b] = tok
                    else:
                        h = m - 8
                        tok = p.op("scalar", lambda E, o=v_sc[:, :],
                                   i=banks[b][:, :], bc=bq_sb[:, m:m + 1]:
                                   E.activation(o, i, AF.Identity, bias=bc),
                                   waits=[mml, tb] +
                                   ([vp_toks[("sc",)]] if ("sc",) in vp_toks else []))
                        for sub in range(4):
                            kt_tok = nch * 4 + sub
                            b2 = sub % 2
                            tt = p.op("tensor", lambda E, o=banksb[b2][:, :],
                                      i=v_sc[:, sub * 128:(sub + 1) * 128]:
                                      E.transpose(o, i, ident[:, :]),
                                      waits=[tok] +
                                      ([vtrans_prev[b2]] if b2 in vtrans_prev else []))
                            tv = p.op("vector", lambda E,
                                      o=v_sb[:, kt_tok * NH * 128 + h * 128:
                                             kt_tok * NH * 128 + (h + 1) * 128],
                                      i=banksb[b2][:, :]: E.tensor_copy(o, i),
                                      waits=[tt])
                            vtrans_prev[b2] = tv
                            vp_toks[(kt_tok, h)] = tv
                        vp_toks[("sc",)] = tv
                        bstate[b] = tv
                xn_cons[pb] = [last_mm]
            qkv_done = p.tokens()

        with nc.sbuf_tensor("abc", [128, SEQ], F32) as abc, \
             nc.sbuf_tensor("mbc", [128, SEQ], F32) as mbc, \
             nc.sbuf_tensor("rbc", [128, SEQ], F32) as rbc, \
             nc.sbuf_tensor("mrow", [1, SEQ], F32) as mrow, \
             nc.sbuf_tensor("rrow", [1, SEQ], F32) as rrow, \
             nc.sbuf_tensor("lsc", [128, 2 * 512], F32) as lsc, \
             nc.sbuf_tensor("esc", [128, 2 * 512], BF16) as esc, \
             nc.sbuf_tensor("stats", [128, 4 + 2 * KTT], F32) as stats, \
             nc.sbuf_tensor("psc", [128, 2 * 512], BF16) as psc, \
             nc.sbuf_tensor("osc", [128, 2 * 512], BF16) as osc:
            p.barrier(qkv_done)
            s_ab = p.sem("s_ab")
            s_os = [p.sem(f"s_os{i}") for i in range(2)]
            mneg_col = stats[:, 4:4 + KTT]
            rrec_col = stats[:, 4 + KTT:4 + 2 * KTT]
            fillc = p.pool_reg(-1e9)
            p1rel = {}
            p2rel = {}
            p2psc = {}
            rowrel = {}
            orel = {}
            ost = {}
            abc_tok = None
            for h in range(NH):
                tab = p.dma("gpsimd", s_ab, abc[:, :],
                            alibi[h:h + 1, :].broadcast_to([128, SEQ]),
                            waits=p.tokens() if h > 0 else [])
                p1stat = {}
                for qt in range(KTT):
                    nkc = qt // 4 + 1
                    tch = None
                    for kc in range(nkc):
                        b = kc % 2
                        tS = p.op("tensor", lambda E, ps=banks[b],
                                  lt=q_sb[:, h * SEQ + qt * 128: h * SEQ + (qt + 1) * 128],
                                  r=k_sb[:, h * SEQ + kc * 512: h * SEQ + (kc + 1) * 512]:
                                  E.matmul(ps[:, :], lt, r, start=True, stop=True),
                                  waits=[p1rel[b]] if b in p1rel else [])
                        tL = p.op("vector", lambda E, o=lsc[:, b * 512:(b + 1) * 512],
                                  i=banks[b][:, :], a=abc[:, kc * 512:(kc + 1) * 512]:
                                  E.scalar_tensor_tensor(o, i, SCALE, a, OP.mult, OP.add),
                                  waits=[tS, tab] + ([p1rel[b]] if b in p1rel else []))
                        if kc == nkc - 1:
                            tL = p.op("gpsimd", lambda E, o=lsc[:, b * 512:(b + 1) * 512],
                                      i=lsc[:, b * 512:(b + 1) * 512],
                                      base=qt * 128 - kc * 512:
                                      E.affine_select(o, i, [[-1, 512]], OP.is_ge,
                                                      fillc["r"], base=base,
                                                      channel_multiplier=1),
                                      waits=[tL])
                        tch = p.op("vector", lambda E, o=stats[:, 2:3],
                                   i=lsc[:, b * 512:(b + 1) * 512]:
                                   E.tensor_reduce(o, i, AX.X, OP.max, negate=True),
                                   waits=[tL] + ([tch] if tch else []))
                        if kc == 0:
                            tch = p.op("vector", lambda E: E.tensor_copy(
                                stats[:, 0:1], stats[:, 2:3]), waits=[tch])
                            tch = p.op("scalar", lambda E, o=esc[:, b * 512:(b + 1) * 512],
                                       i=lsc[:, b * 512:(b + 1) * 512]:
                                       E.activation(o, i, AF.Exp, bias=stats[:, 0:1],
                                                    accum_out=stats[:, 1:2]),
                                       waits=[tch])
                        else:
                            tch = p.op("vector", lambda E: E.tensor_tensor(
                                stats[:, 2:3], stats[:, 2:3], stats[:, 0:1], OP.min),
                                waits=[tch])
                            tch = p.op("vector", lambda E: E.tensor_sub(
                                stats[:, 3:4], stats[:, 2:3], stats[:, 0:1]),
                                waits=[tch])
                            tch = p.op("scalar", lambda E: E.activation(
                                stats[:, 3:4], stats[:, 3:4], AF.Exp), waits=[tch])
                            tch = p.op("vector", lambda E: E.tensor_mul(
                                stats[:, 1:2], stats[:, 1:2], stats[:, 3:4]),
                                waits=[tch])
                            tch = p.op("vector", lambda E: E.tensor_copy(
                                stats[:, 0:1], stats[:, 2:3]), waits=[tch])
                            tch = p.op("scalar", lambda E, o=esc[:, b * 512:(b + 1) * 512],
                                       i=lsc[:, b * 512:(b + 1) * 512]:
                                       E.activation(o, i, AF.Exp, bias=stats[:, 0:1],
                                                    accum_out=stats[:, 2:3]),
                                       waits=[tch])
                            tch = p.op("vector", lambda E: E.tensor_add(
                                stats[:, 1:2], stats[:, 1:2], stats[:, 2:3]),
                                waits=[tch])
                        p1rel[b] = tch
                    t1 = p.op("vector", lambda E, o=mneg_col[:, qt:qt + 1]:
                              E.tensor_copy(o, stats[:, 0:1]), waits=[tch])
                    t2 = p.op("vector", lambda E, o=rrec_col[:, qt:qt + 1],
                              i=stats[:, 1:2]: E.reciprocal(o, i), waits=[t1])
                    p1stat[qt] = t2
                rowt = None
                for qt in range(KTT):
                    bm = 4
                    br = 5
                    mw = [p1stat[qt]]
                    if rowrel.get(bm) is not None:
                        mw.append(rowrel[bm])
                    t_a = p.op("tensor", lambda E, ps=banks[bm], col=mneg_col[:, qt:qt + 1]:
                               E.matmul(ps[0:1, 0:128], col, ident32[:, :],
                                        start=True, stop=True), waits=mw)
                    mw2 = [p1stat[qt]]
                    if rowrel.get(br) is not None:
                        mw2.append(rowrel[br])
                    t_b = p.op("tensor", lambda E, ps=banks[br], col=rrec_col[:, qt:qt + 1]:
                               E.matmul(ps[0:1, 0:128], col, ident32[:, :],
                                        start=True, stop=True), waits=mw2)
                    tev = p.op("vector", lambda E, o=mrow[:, qt * 128:(qt + 1) * 128],
                               i=banks[bm][0:1, 0:128]: E.tensor_copy(o, i), waits=[t_a])
                    rowrel[bm] = tev
                    tev2 = p.op("vector", lambda E, o=rrow[:, qt * 128:(qt + 1) * 128],
                                i=banks[br][0:1, 0:128]: E.tensor_copy(o, i), waits=[t_b])
                    rowrel[br] = tev2
                    rowt = tev2
                bc_done = None
                for si in range(SEQ // 512):
                    bm = 4
                    br = 5
                    tt1 = p.op("tensor", lambda E, ps=banks[bm],
                               r=mrow[:, si * 512:(si + 1) * 512]:
                               E.matmul(ps[:, :], ones_row32[:, :], r, start=True,
                                        stop=True), waits=[rowt, rowrel[bm]])
                    tb1 = p.op("vector", lambda E, o=mbc[:, si * 512:(si + 1) * 512],
                               i=banks[bm][:, :]: E.tensor_copy(o, i), waits=[tt1])
                    rowrel[bm] = tb1
                    tt2 = p.op("tensor", lambda E, ps=banks[br],
                               r=rrow[:, si * 512:(si + 1) * 512]:
                               E.matmul(ps[:, :], ones_row32[:, :], r, start=True,
                                        stop=True), waits=[rowt, rowrel[br]])
                    tb2 = p.op("vector", lambda E, o=rbc[:, si * 512:(si + 1) * 512],
                               i=banks[br][:, :]: E.tensor_copy(o, i), waits=[tt2])
                    rowrel[br] = tb2
                    bc_done = tb2
                for qc in range(NQC):
                    nkt = (qc + 1) * 4
                    ob = 2 + (qc % 2)
                    mml = None
                    for kt in range(nkt):
                        b = kt % 2
                        tS = p.op("tensor", lambda E, ps=banks[b],
                                  lt=k_sb[:, h * SEQ + kt * 128: h * SEQ + (kt + 1) * 128],
                                  r=q_sb[:, h * SEQ + qc * 512: h * SEQ + (qc + 1) * 512]:
                                  E.matmul(ps[:, :], lt, r, start=True, stop=True),
                                  waits=[p2rel[b]] if b in p2rel else [])
                        tT = p.op("vector", lambda E, o=lsc[:, b * 512:(b + 1) * 512],
                                  i=banks[b][:, :], a=mbc[:, qc * 512:(qc + 1) * 512]:
                                  E.scalar_tensor_tensor(o, i, SCALE, a, OP.mult, OP.add),
                                  waits=[tS, bc_done] + ([p2rel[b]] if b in p2rel else []))
                        if kt >= qc * 4:
                            tT = p.op("gpsimd", lambda E, o=lsc[:, b * 512:(b + 1) * 512],
                                      i=lsc[:, b * 512:(b + 1) * 512],
                                      base=qc * 512 - kt * 128:
                                      E.affine_select(o, i, [[1, 512]], OP.is_ge,
                                                      fillc["r"], base=base,
                                                      channel_multiplier=-1),
                                      waits=[tT])
                        tE = p.op("scalar", lambda E, o=psc[:, b * 512:(b + 1) * 512],
                                  i=lsc[:, b * 512:(b + 1) * 512],
                                  bc=acolT[:, h * KTT + kt: h * KTT + kt + 1]:
                                  E.activation(o, i, AF.Exp, bias=bc),
                                  waits=[tT] + ([p2psc[b]] if b in p2psc else []))
                        p2rel[b] = tE
                        vw = [tE]
                        if kt == 0 and orel.get(ob) is not None:
                            vw.append(orel[ob])
                        tPV = p.op("tensor", lambda E, ps=banks[ob],
                                   lt=v_sb[:, kt * NH * 128 + h * 128:
                                           kt * NH * 128 + (h + 1) * 128],
                                   r=psc[:, b * 512:(b + 1) * 512],
                                   st=(kt == 0), sp=(kt == nkt - 1):
                                   E.matmul(ps[:, :], lt, r, start=st, stop=sp),
                                   waits=vw)
                        p2psc[b] = tPV
                        mml = tPV
                    tev = p.op("vector", lambda E, o=osc[:, (qc % 2) * 512:(qc % 2) * 512 + 512],
                               i=banks[ob][:, :], a=rbc[:, qc * 512:(qc + 1) * 512]:
                               E.tensor_tensor(o, i, a, OP.mult),
                               waits=[mml] + ([ost.get(qc % 2)] if ost.get(qc % 2) else []))
                    orel[ob] = tev
                    tst = p.dma("sync", s_os[qc % 2],
                                o_ctx[h * HDIM:(h + 1) * HDIM, qc * 512:(qc + 1) * 512],
                                osc[:, (qc % 2) * 512:(qc % 2) * 512 + 512], waits=[tev])
                    ost[qc % 2] = tst
            p.wait_only("gpsimd", (s_os[0], s_os[0].n))
            p.wait_only("gpsimd", (s_os[1], s_os[1].n))

        blk = stack.enter_context(nc.Block())
        p.run(blk)
    return nc


def build_launch_E():
    nc = bass.Bass("TRN2", target_bir_lowering=False)
    KT = D // 128
    IT = IEXP // 128
    ctx = nc.dram_tensor("ctx", [D, TC], BF16, kind="ExternalInput").ap()
    x1 = nc.dram_tensor("x1", [D, TC], F32, kind="ExternalInput").ap()
    xn1 = nc.dram_tensor("xn1", [D, TC], BF16, kind="ExternalInput").ap()
    woutT = nc.dram_tensor("woutT", [D, D], F32, kind="ExternalInput").ap()
    bout = nc.dram_tensor("bout", [128, KT], F32, kind="ExternalInput").ap()
    gatewT = nc.dram_tensor("gatewT", [D, NEXP], F32, kind="ExternalInput").ap()
    gateb = nc.dram_tensor("gateb", [1, NEXP], F32, kind="ExternalInput").ap()
    ebias = nc.dram_tensor("ebias", [1, NEXP], F32, kind="ExternalInput").ap()
    w1T = nc.dram_tensor("w1T", [NEXP, D, IEXP], F32, kind="ExternalInput").ap()
    b1 = nc.dram_tensor("b1", [128, NEXP * IT], F32, kind="ExternalInput").ap()
    w2T = nc.dram_tensor("w2T", [NEXP, IEXP, D], F32, kind="ExternalInput").ap()
    b2 = nc.dram_tensor("b2", [NEXP, D], F32, kind="ExternalInput").ap()
    o_x2 = nc.dram_tensor("o_x2", [D, TC], F32, kind="ExternalOutput").ap()

    spans = _spans(TC)
    nsp = len(spans)

    with ExitStack() as stack:
        p = Prog(nc, stack)
        sb = lambda n, s, d: stack.enter_context(nc.sbuf_tensor(n, s, d))
        s_in = p.sem("s_in")
        s_c = p.sem("s_c")
        p.init_consts([1e-5, 1.0, -1.0], s_c)
        s1_sb = sb("s1_sb", [128, KT * TC], BF16)
        macc = sb("macc", [128, KT * TC], F32)
        ct_sb = sb("ct_sb", [1, NEXP * TC], BF16)
        c8_sb = sb("c8_sb", [8, TC], BF16)
        gatew_sb = sb("gatew_sb", [128, KT * NEXP], BF16)
        bout_sb = sb("bout_sb", [128, KT], F32)
        b1_sb = sb("b1_sb", [128, NEXP * IT], F32)
        b2_sb = sb("b2_sb", [8, D], BF16)
        gbbc = sb("gbbc", [128, NEXP], F32)
        grow = sb("grow", [1, 2 * NEXP], F32)
        ones_col = sb("ones_col", [128, 1], BF16)
        ones_row32 = sb("ones_row32", [1, 128], F32)
        ones_rowbf = sb("ones_rowbf", [1, 128], BF16)
        ident32 = sb("ident32", [128, 128], F32)
        f32_sc = sb("f32_sc", [128, 512], F32)
        banks = [stack.enter_context(nc.psum_tensor(f"pb{i}", [128, 512], F32))
                 for i in range(8)]
        ws = WStream(p, "we", D)

        co = nc.inline_tensor(np.ones((128, 1), dtype=BF), name="ones_col_c")
        cr = nc.inline_tensor(np.ones((1, 128), dtype=np.float32), name="ones_row_c")
        crb = nc.inline_tensor(np.ones((1, 128), dtype=BF), name="ones_rowbf_c")
        idc32 = nc.inline_tensor(np.eye(128, dtype=np.float32), name="id32_c")
        p.dma("sync", s_in, ones_col[:, :], co.ap())
        p.dma("sync", s_in, ones_row32[:, :], cr.ap())
        p.dma("sync", s_in, ones_rowbf[:, :], crb.ap())
        p.dma("sync", s_in, ident32[:, :], idc32.ap())
        p.dma("sync", s_in, bout_sb[:, :], bout)
        p.dma("sync", s_in, b1_sb[:, :], b1)
        p.dma("sync", s_in, grow[:, 0:NEXP], gateb)
        p.dma("sync", s_in, grow[:, NEXP:2 * NEXP], ebias)
        s_gw = p.sem("s_gw")
        p.dma("gpsimd", s_gw, gatew_sb[:, :].rearrange("p (kt e) -> p kt e", kt=KT),
              _kt_view(gatewT))
        p.dma("gpsimd", s_gw, b2_sb[:, :], b2)
        loads = (s_in, s_in.n)
        gw_tok = (s_gw, s_gw.n)
        for e in ("tensor", "vector", "scalar", "gpsimd"):
            p.wait_only(e, loads)
            p.wait_only(e, gw_tok)

        t0 = p.op("vector", lambda E: E.tensor_add(grow[:, 0:NEXP], grow[:, 0:NEXP],
                                                   grow[:, NEXP:2 * NEXP]),
                  waits=[loads])
        t1 = p.op("tensor", lambda E: E.matmul(banks[7][:, :NEXP], ones_row32[:, :],
                                               grow[:, 0:NEXP], start=True, stop=True),
                  waits=[t0])
        gb_tok = p.op("vector", lambda E: E.tensor_copy(gbbc[:, :], banks[7][:, :NEXP]),
                      waits=[t1])

        with nc.sbuf_tensor("ctx_sb", [128, KT * TC], BF16) as ctx_sb:
            s_cx = p.sem("s_cx")
            tcx = p.dma("sync", s_cx, ctx_sb[:, :].rearrange("p (kt t) -> p kt t", kt=KT),
                        _kt_view(ctx))
            cx_tok = (s_cx, s_cx.n)

            def rhs_cx(kt, off, w):
                return ctx_sb[:, kt * TC + off: kt * TC + off + w]

            bstate = {}

            def evac_s1(m_idx, si, off, w, ps, mmtok):
                return p.op("scalar",
                            lambda E, o=s1_sb[:, m_idx * TC + off: m_idx * TC + off + w],
                            i=ps[:, :w], bc=bout_sb[:, m_idx:m_idx + 1]:
                            E.activation(o, i, AF.Identity, bias=bc), waits=[mmtok])

            proj_stream(p, ws, _kt_view(woutT), list(range(KT)), spans, rhs_cx,
                        banks[:4], bstate, evac_s1, rhs_waits=[cx_tok])
            ph1 = p.tokens()

        with nc.sbuf_tensor("xn1_sb", [128, KT * TC], BF16) as xn1_sb, \
             nc.sbuf_tensor("hbuf", [128, IT * TC], BF16) as hbuf, \
             nc.sbuf_tensor("u_sc0", [128, 512], BF16) as u_sc0, \
             nc.sbuf_tensor("u_sc1", [128, 512], BF16) as u_sc1, \
             nc.sbuf_tensor("cbc", [128, TC], F32) as cbc, \
             nc.sbuf_tensor("r_sc", [128, 11 * NEXP], F32) as r_sc:
            p.barrier(ph1)
            s_xn = p.sem("s_xn")
            txn = p.dma("sync", s_xn, xn1_sb[:, :].rearrange("p (kt t) -> p kt t", kt=KT),
                        _kt_view(xn1), waits=ph1)
            xn_tok = (s_xn, s_xn.n)
            u_sc = [u_sc0, u_sc1]

            def rhs_xn(kt, off, w):
                return xn1_sb[:, kt * TC + off: kt * TC + off + w]

            NTCH = TC // 128
            ct_done = None
            c8_done = None
            bstate = {}
            prev_r = None
            for tch in range(NTCH):
                b = 6 + (tch % 2)
                mmw = [xn_tok, gw_tok, gb_tok]
                if bstate.get(b) is not None:
                    mmw.append(bstate[b])
                mml = None
                for kt in range(KT):
                    fn = (lambda E, ps=banks[b],
                          lt=xn1_sb[:, kt * TC + tch * 128: kt * TC + (tch + 1) * 128],
                          r=gatew_sb[:, kt * NEXP:(kt + 1) * NEXP],
                          st=(kt == 0), sp=(kt == KT - 1):
                          E.matmul(ps[:, :NEXP], lt, r, start=st, stop=sp))
                    if kt == KT - 1:
                        mml = p.op("tensor", fn, waits=mmw)
                    else:
                        p.raw("tensor", fn, waits=mmw)
                    mmw = []
                lg = r_sc[:, 0:NEXP]
                m1 = r_sc[:, NEXP:NEXP + 1]
                k1 = r_sc[:, 2 * NEXP:3 * NEXP]
                p2 = r_sc[:, 3 * NEXP:4 * NEXP]
                m2 = r_sc[:, 4 * NEXP:4 * NEXP + 1]
                k2 = r_sc[:, 5 * NEXP:6 * NEXP]
                g1v = r_sc[:, 6 * NEXP:6 * NEXP + 1]
                g2v = r_sc[:, 6 * NEXP + 1:6 * NEXP + 2]
                dd = r_sc[:, 6 * NEXP + 2:6 * NEXP + 3]
                cc = r_sc[:, 7 * NEXP:8 * NEXP]
                war = [prev_r] if prev_r else []
                t0 = p.op("vector", lambda E, o=lg, i=banks[b][:, :NEXP]:
                          E.tensor_tensor(o, i, gbbc[:, :], OP.add),
                          waits=[mml, gb_tok] + war)
                bstate[b] = t0
                t1 = p.op("vector", lambda E: E.tensor_reduce(m1, lg, AX.X, OP.max),
                          waits=[t0])
                t2 = p.op("vector", lambda E: E.tensor_single_scalar(k1, lg, m1, OP.is_equal),
                          waits=[t1])
                t3 = p.op("vector", lambda E: E.scalar_tensor_tensor(p2, k1, -1e9, lg, OP.mult, OP.add),
                          waits=[t2])
                t4 = p.op("vector", lambda E: E.tensor_reduce(m2, p2, AX.X, OP.max),
                          waits=[t3])
                t5 = p.op("vector", lambda E: E.tensor_single_scalar(k2, p2, m2, OP.is_equal),
                          waits=[t4])
                t6 = p.op("vector", lambda E: E.tensor_sub(dd, m1, m2), waits=[t5])
                t7 = p.op("scalar", lambda E: E.activation(g1v, dd, AF.Sigmoid),
                          waits=[t6] + war)
                t8 = p.op("scalar", lambda E: E.activation(g2v, g1v, AF.Identity,
                                                           scale=-1.0, bias=1.0),
                          waits=[t7])
                t9 = p.op("vector", lambda E: E.tensor_scalar(cc, k1, g1v, None, OP.mult),
                          waits=[t8])
                t10 = p.op("vector", lambda E: E.scalar_tensor_tensor(cc, k2, g2v, cc, OP.mult, OP.add),
                           waits=[t9])
                prev_r = t10
                for ex_i in range(NEXP):
                    b2i = 4 + (ex_i % 2)
                    mw = [t10] + ([bstate.get(b2i)] if bstate.get(b2i) else [])
                    tt = p.op("tensor", lambda E, ps=banks[b2i], col=cc[:, ex_i:ex_i + 1]:
                              E.matmul(ps[0:1, 0:128], col, ident32[:, :],
                                       start=True, stop=True), waits=mw)
                    te = p.op("vector", lambda E,
                              o=ct_sb[0:1, ex_i * TC + tch * 128: ex_i * TC + (tch + 1) * 128],
                              i=banks[b2i][0:1, 0:128]: E.tensor_copy(o, i), waits=[tt])
                    bstate[b2i] = te
                    ct_done = te
                    prev_r = te
                b3 = 6 + (tch % 2)
                tt2 = p.op("tensor", lambda E, ps=banks[b3], i=cc:
                           E.transpose(ps[0:NEXP, 0:128], i, ident32[:, :]),
                           waits=[t10, bstate[b3]])
                te2 = p.op("vector", lambda E, o=c8_sb[:, tch * 128:(tch + 1) * 128],
                           i=banks[b3][0:NEXP, 0:128]: E.tensor_copy(o, i), waits=[tt2])
                bstate[b3] = te2
                c8_done = te2
                prev_r = te2

            macc_toks = []
            for mt in range(KT):
                for si, (off, w) in enumerate(spans):
                    b = 4 + ((mt * nsp + si) % 4)
                    mw = [c8_done, gw_tok]
                    if bstate.get(b) is not None:
                        mw.append(bstate[b])
                    tt = p.op("tensor", lambda E, ps=banks[b],
                              lt=b2_sb[0:8, mt * 128:(mt + 1) * 128],
                              r=c8_sb[0:8, off:off + w], w_=w:
                              E.matmul(ps[:, :w_], lt, r, start=True, stop=True),
                              waits=mw)
                    tok = p.op("vector", lambda E, o=macc[:, mt * TC + off: mt * TC + off + w],
                               i=banks[b][:, :w]: E.tensor_copy(o, i), waits=[tt])
                    bstate[b] = tok
                    macc_toks.append(tok)

            u_cons = [None] * nsp
            cbc_tok = None
            for ex_i in range(NEXP):
                cb_toks = []
                for si, (off, w) in enumerate(spans):
                    b = 6 + (si % 2)
                    mmw = [ct_done] + ([bstate.get(b)] if bstate.get(b) else [])
                    tt = p.op("tensor", lambda E, ps=banks[b],
                              r=ct_sb[0:1, ex_i * TC + off: ex_i * TC + off + w],
                              w_=w: E.matmul(ps[:, :w_], ones_rowbf[:, :], r,
                                             start=True, stop=True),
                              waits=mmw)
                    tev = p.op("scalar", lambda E, o=cbc[:, off:off + w], i=banks[b][:, :w]:
                               E.activation(o, i, AF.Copy),
                               waits=[tt] + ([cbc_tok] if cbc_tok else []))
                    bstate[b] = tev
                    cb_toks.append(tev)
                mtoks, cbc_tok = moe_expert_block(
                    p, ws, hbuf, u_sc, cbc[:, :], rhs_xn, spans, banks, bstate,
                    None, _kt_view(w1T[ex_i]), _kt_view(w2T[ex_i]),
                    IT, KT, IT, KT, macc, False, xn_tok,
                    cb_toks[-1], u_cons, macc_guard=None,
                    b1col=lambda mt, e=ex_i: b1_sb[:, e * IT + mt: e * IT + mt + 1])
                macc_toks += mtoks
            g1_toks = []
            gprev = None
            for kt in range(KT):
                gprev = p.op("scalar", lambda E, o=macc[:, kt * TC:(kt + 1) * TC],
                             i=macc[:, kt * TC:(kt + 1) * TC]:
                             E.activation(o, i, AF.Silu),
                             waits=macc_toks[-2 * nsp:] + ([gprev] if gprev else []) +
                             [macc_toks[kt]] if False else
                             macc_toks + ([gprev] if gprev else []))
                g1_toks.append(gprev)
            ph2 = p.tokens()

        with nc.sbuf_tensor("x1_sc", [128, 2 * TC], F32) as x1_sc, \
             nc.sbuf_tensor("x2_sc", [128, 2 * TC], F32) as x2_sc:
            p.barrier(ph2)
            s_x1 = [p.sem(f"s_x1{i}") for i in range(2)]
            s_st = [p.sem(f"s_st{i}") for i in range(2)]
            prevx = {}
            for kt in range(KT):
                pb = kt % 2
                wx = prevx.get(pb, [])
                tx1 = p.dma("sync", s_x1[pb], x1_sc[:, pb * TC:(pb + 1) * TC],
                            _kt_view(x1)[:, kt, :], waits=wx)
                tm = p.op("vector", lambda E, o=x2_sc[:, pb * TC:(pb + 1) * TC],
                          a=s1_sb[:, kt * TC:(kt + 1) * TC],
                          bq=macc[:, kt * TC:(kt + 1) * TC]:
                          E.tensor_tensor(o, a, bq, OP.mult),
                          waits=[g1_toks[kt]] + wx)
                ta = p.op("vector", lambda E, o=x2_sc[:, pb * TC:(pb + 1) * TC],
                          a=x2_sc[:, pb * TC:(pb + 1) * TC],
                          bq=x1_sc[:, pb * TC:(pb + 1) * TC]:
                          E.tensor_tensor(o, a, bq, OP.add), waits=[tm, tx1])
                tst = p.dma("sync", s_st[pb], _kt_view(o_x2)[:, kt, :],
                            x2_sc[:, pb * TC:(pb + 1) * TC], waits=[ta])
                prevx[pb] = [tst, ta]
            p.wait_only("gpsimd", (s_st[0], s_st[0].n))
            p.wait_only("gpsimd", (s_st[1], s_st[1].n))

        blk = stack.enter_context(nc.Block())
        p.run(blk)
    return nc


def _pm(vec, nt):
    return np.ascontiguousarray(np.asarray(vec, np.float32).reshape(nt, 128).T)


def kernel(**inputs):
    CC = np.ascontiguousarray
    inp = {k: np.asarray(v) for k, v in inputs.items()}
    x0 = inp["L0_attn_hidden_states"].reshape(TOKENS, D)
    KT = D // 128
    ET = E // 128
    IT = IEXP // 128

    winT = CC(inp["L0_attn_in_proj_w"].T)
    winT_u = CC(winT[:, :E])
    winT_g = CC(winT[:, E:])
    convw = CC(inp["L0_attn_conv_w"].reshape(E, KCONV)
               .reshape(ET, 128, KCONV).transpose(1, 0, 2).reshape(128, ET * KCONV))
    convb = _pm(inp["L0_attn_conv_b"], ET)
    xprojT = CC(inp["L0_attn_x_proj_w"].T)
    dtprojT = CC(inp["L0_attn_dt_proj_w"].T)
    dtb = _pm(inp["L0_attn_dt_proj_b"], ET)
    in_maps = []
    for c in range(NCORE):
        r0 = c * TC
        x0h = np.zeros((D, TC + 3), np.float32)
        x0h[:, 3:] = x0[r0:r0 + TC].T
        if c % 4 != 0:
            x0h[:, :3] = x0[r0 - 3:r0].T
        in_maps.append({
            "x0h": CC(x0h), "winT_u": winT_u, "winT_g": winT_g,
            "convw": convw, "convb": convb, "xprojT": xprojT,
            "dtprojT": dtprojT, "dtb": dtb,
        })
    resA = run_bass_kernel_spmd(build_launch_A(), in_maps, core_ids=list(range(NCORE)))
    A = resA.results

    xc_full = np.concatenate([A[c]["o_xc"] for c in range(NCORE)], axis=1)
    dt_full = np.concatenate([A[c]["o_dt"] for c in range(NCORE)], axis=1)
    bcrow = np.concatenate([A[c]["o_bc"] for c in range(NCORE)], axis=1)
    alog_f = inp["L0_attn_A_log"]
    dv_f = inp["L0_attn_D"]
    in_maps = []
    for c in range(NCORE):
        e0 = c * ES
        ETS = ES // 128
        alog_c = CC(alog_f[e0:e0 + ES].reshape(ETS, 128, NST)
                    .transpose(1, 0, 2).reshape(128, ETS * NST))
        in_maps.append({
            "xc_e": CC(xc_full[e0:e0 + ES]), "dt_e": CC(dt_full[e0:e0 + ES]),
            "bcrow": bcrow, "alog": alog_c, "dvec": _pm(dv_f[e0:e0 + ES], ETS),
        })
    resB = run_bass_kernel_spmd(build_launch_B(), in_maps, core_ids=list(range(NCORE)))
    B = resB.results

    y_full = np.concatenate([B[c]["o_y"] for c in range(NCORE)], axis=0)
    woutT0 = CC(inp["L0_attn_out_proj_w"].T)
    gatewT0 = CC(inp["L0_ffn_gate_w"].T)
    shgT = CC(inp["L0_ffn_shared_gate"].T)
    shuT = CC(inp["L0_ffn_shared_up"].T)
    shdT = CC(inp["L0_ffn_shared_down"].T)
    egT = CC(inp["L0_ffn_expert_gate"].transpose(0, 2, 1))
    euT = CC(inp["L0_ffn_expert_up"].transpose(0, 2, 1))
    edT = CC(inp["L0_ffn_expert_down"].transpose(0, 2, 1))
    in_maps = []
    for c in range(NCORE):
        r0 = c * TC
        in_maps.append({
            "y_t": CC(y_full[:, r0:r0 + TC]), "sg": A[c]["o_sg"],
            "xn0": A[c]["o_xn"], "x0": CC(x0[r0:r0 + TC].T),
            "woutT": woutT0, "gatewT": gatewT0, "shgT": shgT, "shuT": shuT,
            "shdT": shdT, "egT": egT, "euT": euT, "edT": edT,
        })
    resC = run_bass_kernel_spmd(build_launch_C(), in_maps, core_ids=list(range(NCORE)))
    C = resC.results

    wqkvT = inp["L1_attn_Wqkv"].T
    bqkv_f = inp["L1_attn_b_qkv"]
    slopes = (1.0 / 2.0 ** (np.arange(1, NHEAD + 1) * (8.0 / NHEAD))).astype(np.float32)
    pos = np.arange(1 - SEQ, 1, dtype=np.float32)
    KTT = SEQ // 128
    in_maps = []
    for c in range(NCORE):
        b = c // 4
        heads = [(c % 4) * 4 + i for i in range(4)]
        cols = []
        bias_m = []
        for part in range(3):
            for h in heads:
                cols.append(wqkvT[:, part * D + h * HDIM: part * D + (h + 1) * HDIM])
                bias_m.append(bqkv_f[part * D + h * HDIM: part * D + (h + 1) * HDIM])
        wq_c = np.concatenate(cols, axis=1)
        KTD = D // 128
        wq_c = CC(wq_c.reshape(KTD, 128, 12, 128).transpose(1, 2, 0, 3)
                  .reshape(128, 12 * KTD * 128))
        bq_c = CC(np.stack(bias_m, 0).T)
        xn1b = CC(np.concatenate([C[4 * b + g]["o_xn1"] for g in range(4)], axis=1))
        alibi_c = CC(pos[None, :] * slopes[heads][:, None])
        acolT = np.zeros((128, 4 * KTT), np.float32)
        for i in range(4):
            for kt in range(KTT):
                acolT[:, i * KTT + kt] = alibi_c[i, kt * 128:(kt + 1) * 128]
        in_maps.append({
            "xn1b": xn1b, "wqkvT": wq_c, "bqkv": bq_c,
            "alibi": alibi_c, "alibi_colT": acolT,
        })
    resD = run_bass_kernel_spmd(build_launch_D(), in_maps, core_ids=list(range(NCORE)))
    Dr = resD.results

    woutT1 = CC(inp["L1_attn_Wout"].T)
    bout = _pm(inp["L1_attn_b_out"], KT)
    gatewT1 = CC(inp["L1_ffn_gate_w"].T)
    gateb = CC(inp["L1_ffn_gate_b"].reshape(1, NEXP))
    ebias = CC(inp["L1_ffn_bias"].reshape(1, NEXP))
    w1T = CC(inp["L1_ffn_w1"].transpose(0, 2, 1))
    b1f = inp["L1_ffn_b1"]
    b1 = np.zeros((128, NEXP * IT), np.float32)
    for e in range(NEXP):
        for mt in range(IT):
            b1[:, e * IT + mt] = b1f[e, mt * 128:(mt + 1) * 128]
    w2T = CC(inp["L1_ffn_w2"].transpose(0, 2, 1))
    b2 = CC(inp["L1_ffn_b2"])
    ctxb = [np.concatenate([Dr[4 * b + g]["o_ctx"] for g in range(4)], axis=0)
            for b in range(BATCH)]
    in_maps = []
    for c in range(NCORE):
        b = c // 4
        t0 = (c % 4) * TC
        in_maps.append({
            "ctx": CC(ctxb[b][:, t0:t0 + TC]), "x1": C[c]["o_x1"],
            "xn1": C[c]["o_xn1"], "woutT": woutT1, "bout": bout,
            "gatewT": gatewT1, "gateb": gateb, "ebias": ebias,
            "w1T": w1T, "b1": b1, "w2T": w2T, "b2": b2,
        })
    resE = run_bass_kernel_spmd(build_launch_E(), in_maps, core_ids=list(range(NCORE)))
    Er = resE.results

    out = np.concatenate([np.asarray(Er[c]["o_x2"]).T for c in range(NCORE)], axis=0)
    return out.reshape(BATCH, SEQ, D).astype(np.float32)


# version 03
# speedup vs baseline: 1.0795x; 1.0795x over previous
import numpy as np
import ml_dtypes
from contextlib import ExitStack

import concourse.bass as bass
import concourse.mybir as mybir
from concourse.bass_utils import run_bass_kernel_spmd

F32 = mybir.dt.float32
BF16 = mybir.dt.bfloat16
AF = mybir.ActivationFunctionType
OP = mybir.AluOpType
AX = mybir.AxisListType
BF = ml_dtypes.bfloat16

ENGINES = ("sync", "gpsimd", "tensor", "vector", "scalar")

D = 2048
E = 4096
R = 128
NST = 16
KCONV = 4
NHEAD = 16
HDIM = 128
BATCH = 2
SEQ = 4096
TOKENS = BATCH * SEQ
NCORE = 8
TC = TOKENS // NCORE
ES = E // NCORE
IEXP = 1408
ISH = 2816
NEXP = 8


class Sem:
    __slots__ = ("h", "n")

    def __init__(self, h):
        self.h = h
        self.n = 0


class Prog:

    def __init__(self, nc, stack):
        self.nc = nc
        self.stack = stack
        self.q = {e: [] for e in ENGINES}
        self.esem = {e: self.sem(f"es_{e}") for e in ("tensor", "vector", "scalar", "gpsimd")}
        self.waited = {e: {} for e in ENGINES}

    def sem(self, name):
        return Sem(self.stack.enter_context(self.nc.semaphore(name)))

    def init_consts(self, vals, sem):
        arr = np.tile(np.asarray(vals, np.float32)[None, :], (128, 1))
        dr = self.nc.inline_tensor(arr, name="cconsts")
        tile = self.stack.enter_context(
            self.nc.sbuf_tensor("cconsts_sb", [128, len(vals)], F32))
        tok = self.dma("sync", sem, tile[:, :], dr.ap())
        for j, v in enumerate(vals):
            self.nc.const_aps.register(F32, float(v), tile[:, j:j + 1])
        for e in ("tensor", "vector", "scalar", "gpsimd"):
            self.wait_only(e, tok)
        return tok

    def _wait(self, eng, tok):
        if tok is None:
            return
        sem, val = tok
        if val <= 0:
            return
        w = self.waited[eng]
        if w.get(sem, 0) >= val:
            return
        w[sem] = val
        self.q[eng].append(lambda Eng, s=sem, v=val: Eng.wait_ge(s.h, v))

    @staticmethod
    def _reduce(waits):
        best = {}
        for t in waits:
            if t is None:
                continue
            s, v = t
            if best.get(id(s), (None, -1))[1] < v:
                best[id(s)] = (s, v)
        return best.values()

    def op(self, eng, fn, waits=()):
        for t in self._reduce(waits):
            self._wait(eng, t)
        s = self.esem[eng]
        s.n += 1

        def emit(Eng, fn=fn, s=s):
            fn(Eng).then_inc(s.h, 1)

        self.q[eng].append(emit)
        return (s, s.n)

    def raw(self, eng, fn, waits=()):
        for t in self._reduce(waits):
            self._wait(eng, t)
        self.q[eng].append(fn)

    def dma(self, eng, sem, out, in_, waits=()):
        for t in self._reduce(waits):
            self._wait(eng, t)
        sem.n += 16

        def emit(Eng, out=out, in_=in_, sem=sem):
            Eng.dma_start(out=out, in_=in_).then_inc(sem.h, 16)

        self.q[eng].append(emit)
        return (sem, sem.n)

    def wait_only(self, eng, tok):
        self._wait(eng, tok)

    def tokens(self):
        return [(s, s.n) for s in self.esem.values() if s.n > 0]

    def barrier(self, toks):
        for e in ENGINES:
            for t in toks:
                self.wait_only(e, t)

    def pool_reg(self, val):
        cell = {}

        def setup(Eng, cell=cell, val=val):
            r = Eng.alloc_register("fill_shared")
            Eng.reg_mov(r, val)
            cell["r"] = r

        self.q["gpsimd"].append(setup)
        return cell

    def run(self, block):
        for e in ENGINES:
            def body(Eng, ops=tuple(self.q[e])):
                for f in ops:
                    f(Eng)
            getattr(block, e)(body)


def _spans(n, w=512):
    out = []
    off = 0
    while off < n:
        out.append((off, min(w, n - off)))
        off += min(w, n - off)
    return out


def _kt_view(ap2d, p=128):
    return ap2d.rearrange("(kt p) m -> p kt m", p=p)


class WStream:

    def __init__(self, p, name, kdim, mtile=128, nbuf=2):
        self.p = p
        self.kt_n = kdim // 128
        self.mtile = mtile
        self.bufs = [p.stack.enter_context(
            p.nc.sbuf_tensor(f"{name}{i}", [128, self.kt_n * mtile], BF16))
            for i in range(nbuf)]
        self.sems = [p.sem(f"{name}_s{i}") for i in range(nbuf)]
        self.nbuf = nbuf
        self.uses = 0
        self.last_cons = [None] * nbuf

    def load(self, src3d, kt_n=None):
        k = kt_n or self.kt_n
        assert k <= self.kt_n
        i = self.uses % self.nbuf
        self.uses += 1
        dst = self.bufs[i][:, 0:k * self.mtile].rearrange("p (kt m) -> p kt m", kt=k)
        tok = self.p.dma("gpsimd", self.sems[i], dst, src3d,
                         waits=[self.last_cons[i]] if self.last_cons[i] else [])
        return i, tok

    def consumed(self, i, tok):
        self.last_cons[i] = tok

    def tile(self, i, kt):
        return self.bufs[i][:, kt * self.mtile:(kt + 1) * self.mtile]


def proj_stream(p, ws, wt3d, mts, spans, rhs_fn, banks, bstate, evac_fn,
                rhs_waits=(), extra_mm_waits=()):
    kt_n = ws.kt_n
    nb = len(banks)
    bi = 0
    for m_idx in mts:
        bufi, wtok = ws.load(wt3d[:, :, m_idx * ws.mtile:(m_idx + 1) * ws.mtile])
        mm_last = None
        for si, (off, w) in enumerate(spans):
            b = bi % nb
            bi += 1
            waits = [wtok] + list(rhs_waits) + list(extra_mm_waits)
            if bstate.get(b) is not None:
                waits.append(bstate[b])
            for kt in range(kt_n):
                fn = (lambda E, ps=banks[b], wt=ws.tile(bufi, kt), r=rhs_fn(kt, off, w),
                      st=(kt == 0), sp=(kt == kt_n - 1), w_=w:
                      E.matmul(ps[:, :w_], wt, r, start=st, stop=sp))
                if kt == kt_n - 1:
                    mm_last = p.op("tensor", fn, waits=waits)
                else:
                    p.raw("tensor", fn, waits=waits)
                waits = []
            rel = evac_fn(m_idx, si, off, w, banks[b], mm_last)
            bstate[b] = rel
        ws.consumed(bufi, mm_last)


def layernorm_fm(p, x_sb, xn_sb, sq_sc, rows, banks, consts, ncols, kt_n,
                 x_ready, bc_banks=(6, 7)):
    spans = []
    off = 0
    while off < ncols:
        w = min(512, ncols - off)
        spans.append((off, w))
        off += w
    nsp = len(spans)
    assert nsp <= 3
    oc, orow = consts["ones_col"], consts["ones_row32"]
    f32_sc = consts["f32_sc"]

    sum_tok = [None] * nsp
    sq_guard = None
    for kt in range(kt_n):
        xcol = x_sb[:, kt * ncols:(kt + 1) * ncols]
        tsq = p.op("scalar", lambda E, o=sq_sc[:, :ncols], i=xcol:
                   E.activation(o, i, AF.Square),
                   waits=[x_ready] + ([sq_guard] if sq_guard else []))
        for si, (o, w) in enumerate(spans):
            p.raw("tensor", lambda E, b=banks[si], i=x_sb[:, kt * ncols + o: kt * ncols + o + w],
                  st=(kt == 0), sp=(kt == kt_n - 1), w_=w, oc=oc:
                  E.matmul(b[0:1, :w_], oc, i, start=st, stop=sp),
                  waits=[x_ready] if (kt == 0 and si == 0) else [])
            fn = (lambda E, b=banks[3 + si], i=sq_sc[:, o:o + w], st=(kt == 0),
                  sp=(kt == kt_n - 1), w_=w, oc=oc:
                  E.matmul(b[0:1, :w_], oc, i, start=st, stop=sp))
            if si == nsp - 1:
                tok = p.op("tensor", fn, waits=[tsq])
                sq_guard = tok
                if kt == kt_n - 1:
                    sum_tok = [tok] * nsp
            else:
                p.raw("tensor", fn, waits=[tsq])

    sx = rows[:, 0 * ncols:1 * ncols]
    ssq = rows[:, 1 * ncols:2 * ncols]
    mean = rows[:, 2 * ncols:3 * ncols]
    m2 = rows[:, 3 * ncols:4 * ncols]
    var = rows[:, 4 * ncols:5 * ncols]
    rstd = rows[:, 5 * ncols:6 * ncols]
    ev = None
    for si, (o, w) in enumerate(spans):
        ev = p.op("vector", lambda E, o1=sx[:, o:o + w], i=banks[si][0:1, :w]:
                  E.tensor_copy(o1, i), waits=[sum_tok[si]] + ([ev] if ev else []))
        ev = p.op("vector", lambda E, o1=ssq[:, o:o + w], i=banks[3 + si][0:1, :w]:
                  E.tensor_copy(o1, i), waits=[ev])
    tm = p.op("vector", lambda E: E.tensor_scalar_mul(mean, sx, 1.0 / D), waits=[ev])
    tm2 = p.op("scalar", lambda E: E.activation(m2, mean, AF.Square), waits=[tm])
    tv = p.op("vector", lambda E: E.scalar_tensor_tensor(
        var, ssq, 1.0 / D, m2, OP.mult, OP.subtract), waits=[tm2, tm])
    tsd = p.op("scalar", lambda E: E.activation(rstd, var, AF.Sqrt, bias=1e-5),
               waits=[tv])
    trs = p.op("vector", lambda E: E.reciprocal(rstd, rstd), waits=[tsd])

    bmu = banks[bc_banks[0]]
    brs = banks[bc_banks[1]]
    prev_bc = None
    prev_sc = None
    xn_tok = None
    for si, (o, w) in enumerate(spans):
        tbm = p.op("tensor", lambda E, r=mean[:, o:o + w], w_=w:
                   E.matmul(bmu[:, :w_], orow, r, start=True, stop=True),
                   waits=[tm] + ([prev_bc] if prev_bc else []))
        tbr = p.op("tensor", lambda E, r=rstd[:, o:o + w], w_=w:
                   E.matmul(brs[:, :w_], orow, r, start=True, stop=True),
                   waits=[trs] + ([prev_bc] if prev_bc else []))
        for kt in range(kt_n):
            xcol = x_sb[:, kt * ncols + o: kt * ncols + o + w]
            ta = p.op("vector", lambda E, o1=f32_sc[:, :w], i=xcol, w_=w:
                      E.scalar_tensor_tensor(o1, i, 1.0, bmu[:, :w_], OP.mult, OP.subtract),
                      waits=[tbm, x_ready] + ([prev_sc] if prev_sc else []))
            xn_tok = p.op("vector", lambda E, o1=xn_sb[:, kt * ncols + o: kt * ncols + o + w],
                          i=f32_sc[:, :w], w_=w:
                          E.tensor_tensor(o1, i, brs[:, :w_], OP.mult),
                          waits=[tbr, ta])
            prev_sc = xn_tok
        prev_bc = xn_tok
    return xn_tok


def build_launch_A():
    nc = bass.Bass("TRN2", target_bir_lowering=False)
    TH = TC + 3
    x0h = nc.dram_tensor("x0h", [D, TH], F32, kind="ExternalInput").ap()
    winT_u = nc.dram_tensor("winT_u", [D, E], F32, kind="ExternalInput").ap()
    winT_g = nc.dram_tensor("winT_g", [D, E], F32, kind="ExternalInput").ap()
    ET0 = E // 128
    convw = nc.dram_tensor("convw", [128, ET0 * KCONV], F32, kind="ExternalInput").ap()
    convb = nc.dram_tensor("convb", [128, ET0], F32, kind="ExternalInput").ap()
    xprojT = nc.dram_tensor("xprojT", [E, 160], F32, kind="ExternalInput").ap()
    dtprojT = nc.dram_tensor("dtprojT", [R, E], F32, kind="ExternalInput").ap()
    dtb = nc.dram_tensor("dtb", [128, ET0], F32, kind="ExternalInput").ap()

    o_xc = nc.dram_tensor("o_xc", [E, TC], BF16, kind="ExternalOutput").ap()
    o_sg = nc.dram_tensor("o_sg", [E, TC], BF16, kind="ExternalOutput").ap()
    o_dt = nc.dram_tensor("o_dt", [E, TC], F32, kind="ExternalOutput").ap()
    o_bc = nc.dram_tensor("o_bc", [32, TC], F32, kind="ExternalOutput").ap()
    o_xn = nc.dram_tensor("o_xn", [D, TC], BF16, kind="ExternalOutput").ap()

    KT = D // 128
    ET = E // 128
    NSLAB = 4
    MTS = ET // NSLAB

    with ExitStack() as stack:
        p = Prog(nc, stack)
        sb = lambda n, s, d: stack.enter_context(nc.sbuf_tensor(n, s, d))
        x_sb = sb("x_sb", [128, KT * TH], BF16)
        xn_sb = sb("xn_sb", [128, KT * TH], BF16)
        sq_sc = sb("sq_sc", [128, TH], BF16)
        rows = sb("rows", [1, 6 * TH], F32)
        f32_sc = sb("f32_sc", [128, 512], F32)
        u_slab = sb("u_slab", [128, MTS * TH], BF16)
        xc_slab = sb("xc_slab", [128, MTS * TC], BF16)
        conv_sc = [sb(f"conv_sc{i}", [128, TC], F32) for i in range(2)]
        convw_sb = sb("convw_sb", [128, ET * KCONV], F32)
        convb_sb = sb("convb_sb", [128, ET], F32)
        dtb_sb = sb("dtb_sb", [128, ET], F32)
        xproj_sb = sb("xproj_sb", [128, ET * 160], BF16)
        dtproj_sb = sb("dtproj_sb", [128, ET * 128], BF16)
        dtlow_sb = sb("dtlow_sb", [128, TC], BF16)
        bc_sb = sb("bc_sb", [32, TC], F32)
        dt_sc = sb("dt_sc", [128, TC], F32)
        sp_scA = sb("sp_scA", [128, 512], F32)
        sp_scB = sb("sp_scB", [128, 512], F32)
        ones_col = sb("ones_col", [128, 1], BF16)
        ones_row32 = sb("ones_row32", [1, 128], F32)
        banks = [stack.enter_context(nc.psum_tensor(f"pb{i}", [128, 512], F32))
                 for i in range(8)]

        s_in = p.sem("s_in")
        s_st = p.sem("s_st")
        s_wx = p.sem("s_wx")
        s_x = p.sem("s_x")
        s_c = p.sem("s_c")
        p.init_consts([1e-5, 1.0, -1.0], s_c)
        ws = WStream(p, "wu", D)

        co = nc.inline_tensor(np.ones((128, 1), dtype=BF), name="ones_col_c")
        cr = nc.inline_tensor(np.ones((1, 128), dtype=np.float32), name="ones_row_c")

        p.dma("gpsimd", s_x, x_sb[:, :].rearrange("p (kt t) -> p kt t", kt=KT),
              _kt_view(x0h))
        p.dma("sync", s_in, convw_sb[:, :], convw)
        p.dma("sync", s_in, convb_sb[:, :], convb)
        p.dma("sync", s_in, dtb_sb[:, :], dtb)
        p.dma("sync", s_in, ones_col[:, :], co.ap())
        p.dma("sync", s_in, ones_row32[:, :], cr.ap())
        p.dma("gpsimd", s_wx, xproj_sb[:, :].rearrange("p (et m) -> p et m", et=ET),
              xprojT.rearrange("(et p) m -> p et m", p=128))
        p.dma("gpsimd", s_wx, dtproj_sb[:, :], dtprojT)
        loads = (s_in, s_in.n)
        wloads = (s_wx, s_wx.n)
        for e in ("tensor", "vector", "scalar", "gpsimd"):
            p.wait_only(e, loads)
        consts = {"ones_col": ones_col[:, :], "ones_row32": ones_row32[:, :],
                  "f32_sc": f32_sc}

        xn_done = layernorm_fm(p, x_sb, xn_sb, sq_sc, rows, banks, consts, TH, KT,
                               x_ready=(s_x, 16), bc_banks=(6, 7))

        spans_h = _spans(TH)
        spans_t = _spans(TC)

        def rhs_xn(kt, off, w):
            return xn_sb[:, kt * TH + off: kt * TH + off + w]

        def rhs_xn_tok(kt, off, w):
            return xn_sb[:, kt * TH + 3 + off: kt * TH + 3 + off + w]

        bstate = {}
        winTu3 = _kt_view(winT_u)
        winTg3 = _kt_view(winT_g)

        conv_done_prev = None
        xdbl_prev = None
        st_xc_prev = None
        xdbl_final = [None, None]
        for s in range(NSLAB):
            evac_toks = {}

            def evac_u(m_idx, si, off, w, ps, mmtok, s=s, evac_toks=evac_toks):
                mt = m_idx - s * MTS
                extra = [conv_done_prev] if conv_done_prev else []
                tok = p.op("scalar",
                           lambda E, o=u_slab[:, mt * TH + off: mt * TH + off + w],
                           i=ps[:, :w]: E.activation(o, i, AF.Copy),
                           waits=[mmtok] + extra)
                evac_toks.setdefault(mt, []).append(tok)
                return tok

            proj_stream(p, ws, winTu3, list(range(s * MTS, (s + 1) * MTS)), spans_h,
                        rhs_xn, banks[:4], bstate, evac_u, rhs_waits=[xn_done])

            conv_toks = []
            for mt in range(MTS):
                et = s * MTS + mt
                ucols = u_slab[:, mt * TH:(mt + 1) * TH]
                t0 = p.op("vector", lambda E, o=conv_sc[0][:, :], i=ucols[:, 0:TC],
                          sc=convw_sb[:, et * KCONV: et * KCONV + 1]:
                          E.tensor_scalar(o, i, sc, None, OP.mult),
                          waits=evac_toks[mt] + [loads] +
                          ([conv_toks[-1]] if conv_toks else
                           ([conv_done_prev] if conv_done_prev else [])))
                cur = 0
                for j in range(1, KCONV):
                    nxt = 1 - cur
                    t0 = p.op("vector", lambda E, o=conv_sc[nxt][:, :],
                              i=ucols[:, j:j + TC],
                              sc=convw_sb[:, et * KCONV + j: et * KCONV + j + 1],
                              a=conv_sc[cur][:, :]:
                              E.scalar_tensor_tensor(o, i, sc, a, OP.mult, OP.add),
                              waits=[t0])
                    cur = nxt
                txc = p.op("scalar", lambda E, o=xc_slab[:, mt * TC:(mt + 1) * TC],
                           i=conv_sc[cur][:, :], b=convb_sb[:, et:et + 1]:
                           E.activation(o, i, AF.Silu, bias=b),
                           waits=[t0] +
                           ([st_xc_prev] if st_xc_prev else []) +
                           ([xdbl_prev] if xdbl_prev else []))
                conv_toks.append(txc)
            conv_done_prev = conv_toks[-1]

            for si, (off, w) in enumerate(spans_t):
                for mt in range(MTS):
                    et = s * MTS + mt
                    st = (s == 0 and mt == 0)
                    sp = (s == NSLAB - 1 and mt == MTS - 1)
                    wts = [conv_toks[mt], wloads]
                    fn1 = (lambda E, ps=banks[4 + si],
                           lt=xproj_sb[:, et * 160: et * 160 + 128],
                           r=xc_slab[:, mt * TC + off: mt * TC + off + w],
                           st=st, sp=sp, w_=w:
                           E.matmul(ps[:, :w_], lt, r, start=st, stop=sp))
                    fn2 = (lambda E, ps=banks[6 + si],
                           lt=xproj_sb[:, et * 160 + 128: et * 160 + 160],
                           r=xc_slab[:, mt * TC + off: mt * TC + off + w],
                           st=st, sp=sp, w_=w:
                           E.matmul(ps[0:32, :w_], lt, r, start=st, stop=sp))
                    last_of_slab = (mt == MTS - 1 and si == len(spans_t) - 1)
                    if sp:
                        xdbl_final[si] = p.op("tensor", fn1, waits=wts)
                        tok2 = p.op("tensor", fn2, waits=wts)
                        xdbl_prev = tok2
                        if si == len(spans_t) - 1:
                            xdbl_final.append(tok2)
                    elif last_of_slab:
                        p.raw("tensor", fn1, waits=wts)
                        xdbl_prev = p.op("tensor", fn2, waits=wts)
                    else:
                        p.raw("tensor", fn1, waits=wts)
                        p.raw("tensor", fn2, waits=wts)

            st_xc_prev = p.dma(
                "sync", s_st,
                o_xc.rearrange("(et p) t -> p et t", p=128)[:, s * MTS:(s + 1) * MTS, :],
                xc_slab[:, :].rearrange("p (et t) -> p et t", et=MTS),
                waits=conv_toks)

        st_sg_prev = None
        for s in range(NSLAB):
            g_toks = []

            def evac_g2(m_idx, si, off, w, ps, mmtok, s=s, g_toks=g_toks):
                mt = m_idx - s * MTS
                extra = []
                if s == 0 and conv_done_prev is not None:
                    extra.append(conv_done_prev)
                if st_sg_prev is not None:
                    extra.append(st_sg_prev)
                tok = p.op("scalar",
                           lambda E, o=u_slab[:, mt * TC + off: mt * TC + off + w],
                           i=ps[:, :w]: E.activation(o, i, AF.Silu),
                           waits=[mmtok] + extra)
                g_toks.append(tok)
                return tok

            proj_stream(p, ws, winTg3, list(range(s * MTS, (s + 1) * MTS)), spans_t,
                        rhs_xn_tok, banks[:4], bstate, evac_g2, rhs_waits=[xn_done])
            st_sg_prev = p.dma(
                "sync", s_st,
                o_sg.rearrange("(et p) t -> p et t", p=128)[:, s * MTS:(s + 1) * MTS, :],
                u_slab[:, 0:MTS * TC].rearrange("p (et t) -> p et t", et=MTS),
                waits=g_toks)

        dtlow_toks = []
        bc_toks = []
        for si, (off, w) in enumerate(spans_t):
            t1 = p.op("scalar", lambda E, o=dtlow_sb[:, off:off + w],
                      i=banks[4 + si][:, :w]: E.activation(o, i, AF.Copy),
                      waits=[xdbl_final[si]])
            t2 = p.op("vector", lambda E, o=bc_sb[:, off:off + w],
                      i=banks[6 + si][0:32, :w]: E.tensor_copy(o, i),
                      waits=[xdbl_final[-1]])
            dtlow_toks.append(t1)
            bc_toks.append(t2)
        p.dma("sync", s_st, o_bc, bc_sb[:, :], waits=dtlow_toks + bc_toks)

        st_dt_prev = None
        prev_sp = None
        for me in range(ET):
            sp_toks = []
            for si, (off, w) in enumerate(spans_t):
                b = si
                waits = dtlow_toks + [wloads]
                if bstate.get(b) is not None:
                    waits.append(bstate[b])
                tok = p.op("tensor", lambda E, ps=banks[b],
                           lt=dtproj_sb[:, me * 128:(me + 1) * 128],
                           r=dtlow_sb[:, off:off + w], w_=w:
                           E.matmul(ps[:, :w_], lt, r, start=True, stop=True),
                           waits=waits)
                bcol = dtb_sb[:, me:me + 1]
                war = [prev_sp] if prev_sp else []
                ta = p.op("scalar", lambda E, o=sp_scA[:, :w], i=banks[b][:, :w], bc2=bcol:
                          E.activation(o, i, AF.Relu, bias=bc2), waits=[tok] + war)
                tb2 = p.op("scalar", lambda E, o=sp_scB[:, :w], i=banks[b][:, :w], bc2=bcol:
                           E.activation(o, i, AF.Abs, bias=bc2), waits=[tok] + war)
                tc2 = p.op("scalar", lambda E, o=sp_scB[:, :w], i=sp_scB[:, :w]:
                           E.activation(o, i, AF.Exp, scale=-1.0), waits=[tb2])
                td2 = p.op("scalar", lambda E, o=sp_scB[:, :w], i=sp_scB[:, :w]:
                           E.activation(o, i, AF.Ln, bias=1.0), waits=[tc2])
                te = p.op("vector", lambda E, o=dt_sc[:, off:off + w], a=sp_scA[:, :w],
                          bb=sp_scB[:, :w]: E.tensor_add(o, a, bb),
                          waits=[ta, td2] + ([st_dt_prev] if st_dt_prev else []))
                prev_sp = te
                sp_toks.append(te)
                bstate[b] = te
            st_dt_prev = p.dma("sync", s_st, o_dt[me * 128:(me + 1) * 128, :],
                               dt_sc[:, :], waits=sp_toks)

        xn_tok_view = xn_sb[:, :].rearrange("p (kt t) -> p kt t", kt=KT)[:, :, 3:TH]
        p.dma("sync", s_st, o_xn.rearrange("(kt p) t -> p kt t", p=128),
              xn_tok_view, waits=[xn_done])

        p.wait_only("gpsimd", (s_st, s_st.n))

        blk = stack.enter_context(nc.Block())
        p.run(blk)
    return nc


def build_launch_B():
    nc = bass.Bass("TRN2", target_bir_lowering=False)
    ETS = ES // 128
    TCH = globals().get("TCH", 2048)
    NCH_SEQ = SEQ // TCH

    xc_e = nc.dram_tensor("xc_e", [ES, TOKENS], BF16, kind="ExternalInput").ap()
    dt_e = nc.dram_tensor("dt_e", [ES, TOKENS], F32, kind="ExternalInput").ap()
    bcrow = nc.dram_tensor("bcrow", [32, TOKENS], F32, kind="ExternalInput").ap()
    alog = nc.dram_tensor("alog", [128, ETS * NST], F32, kind="ExternalInput").ap()
    dvec = nc.dram_tensor("dvec", [128, ETS], F32, kind="ExternalInput").ap()
    o_y = nc.dram_tensor("o_y", [ES, TOKENS], BF16, kind="ExternalOutput").ap()

    with ExitStack() as stack:
        p = Prog(nc, stack)
        sb = lambda n, s, d: stack.enter_context(nc.sbuf_tensor(n, s, d))
        s_in = p.sem("s_in")
        s_c = p.sem("s_c")
        s_st = p.sem("s_st")
        p.init_consts([1e-5, 1.0, -1.0], s_c)

        dt_sb = sb("dt_sb", [128, ETS * TCH], F32)
        xc_sb = sb("xc_sb", [128, ETS * TCH], BF16)
        dtu_sb = sb("dtu_sb", [128, ETS * TCH], BF16)
        bbc_sb = [sb(f"bbc{i}", [128, TCH], BF16) for i in range(2)]
        cbc_sb = [sb(f"cbc{i}", [128, TCH], BF16) for i in range(2)]
        dA_sb = [sb(f"dA{i}", [128, TCH], F32) for i in range(2)]
        X_sb = [sb(f"X{i}", [128, TCH], BF16) for i in range(2)]
        s_sb = [sb(f"s{i}", [128, TCH], BF16) for i in range(2)]
        sC_sb = [sb(f"sC{i}", [128, TCH], BF16) for i in range(2)]
        y_sb = [sb(f"y{i}", [128, TCH], BF16) for i in range(2)]
        slast_sb = sb("slast", [128, ETS * NST], BF16)
        A_sb = sb("A_sb", [128, ETS * NST], F32)
        Dv_sb = sb("Dv_sb", [128, ETS], F32)
        ident = sb("ident", [128, 128], BF16)
        banks = [stack.enter_context(nc.psum_tensor(f"pb{i}", [128, 512], F32))
                 for i in range(8)]

        idc = nc.inline_tensor(np.eye(128, dtype=BF), name="ident_c")
        p.dma("sync", s_in, ident[:, :], idc.ap())
        p.dma("sync", s_in, A_sb[:, :], alog)
        p.dma("sync", s_in, Dv_sb[:, :], dvec)
        loads = (s_in, s_in.n)
        for e in ("tensor", "vector", "scalar", "gpsimd"):
            p.wait_only(e, loads)

        ta = p.op("scalar", lambda E: E.activation(A_sb[:, :], A_sb[:, :], AF.Exp),
                  waits=[loads])
        ta = p.op("vector", lambda E: E.tensor_scalar_mul(A_sb[:, :], A_sb[:, :], -1.0),
                  waits=[ta])

        s_dt = p.sem("s_dt")
        s_xc = p.sem("s_xc")
        s_bc = [p.sem(f"s_bc{i}") for i in range(2)]
        s_cc = [p.sem(f"s_cc{i}") for i in range(2)]
        s_y = [p.sem(f"s_y{i}") for i in range(2)]

        dt3 = dt_e.rearrange("(et p) t -> p et t", p=128)
        xc3 = xc_e.rearrange("(et p) t -> p et t", p=128)
        oy3 = o_y.rearrange("(et p) t -> p et t", p=128)

        spans = _spans(TCH)
        chunk_cons = None
        bc_cons = [None, None]
        buf_cons = {}
        bank_rel = {}
        prev_scan = None
        st_prev = {}

        ci = 0
        for b in range(BATCH):
            for h in range(NCH_SEQ):
                c0 = b * SEQ + h * TCH
                wdt = list(chunk_cons) if chunk_cons else []
                tdt = p.dma("sync", s_dt, dt_sb[:, :].rearrange("p (et t) -> p et t", et=ETS),
                            dt3[:, :, c0:c0 + TCH], waits=wdt)
                txc = p.dma("sync", s_xc, xc_sb[:, :].rearrange("p (et t) -> p et t", et=ETS),
                            xc3[:, :, c0:c0 + TCH], waits=wdt)
                dtu_toks = []
                for et in range(ETS):
                    tok = p.op("vector", lambda E, o=dtu_sb[:, et * TCH:(et + 1) * TCH],
                               a=dt_sb[:, et * TCH:(et + 1) * TCH],
                               bq=xc_sb[:, et * TCH:(et + 1) * TCH]:
                               E.tensor_tensor(o, a, bq, OP.mult),
                               waits=[tdt, txc] + (chunk_cons if chunk_cons else []))
                    dtu_toks.append(tok)
                y_evac_toks = []
                for et in range(ETS):
                    bset = [banks[(et % 2) * 4 + i] for i in range(4)]
                    mm_tok = None
                    for n in range(NST):
                        pb = (et * NST + n) % 2
                        an = et * NST + n
                        wb = [bc_cons[pb]] if bc_cons[pb] else []
                        tb = p.dma("gpsimd", s_bc[pb], bbc_sb[pb][:, :],
                                   bcrow[n:n + 1, c0:c0 + TCH].broadcast_to([128, TCH]),
                                   waits=wb)
                        tc = p.dma("gpsimd", s_cc[pb], cbc_sb[pb][:, :],
                                   bcrow[16 + n:17 + n, c0:c0 + TCH].broadcast_to([128, TCH]),
                                   waits=wb)
                        wda = [tdt, ta, buf_cons.get(("dA", pb))]
                        tda = p.op("scalar", lambda E, o=dA_sb[pb][:, :],
                                   i=dt_sb[:, et * TCH:(et + 1) * TCH],
                                   sc=A_sb[:, an:an + 1]:
                                   E.activation(o, i, AF.Exp, scale=sc),
                                   waits=[t for t in wda if t])
                        wx = [dtu_toks[et], tb, buf_cons.get(("X", pb))]
                        tx = p.op("vector", lambda E, o=X_sb[pb][:, :],
                                  a=dtu_sb[:, et * TCH:(et + 1) * TCH], bq=bbc_sb[pb][:, :]:
                                  E.tensor_tensor(o, a, bq, OP.mult),
                                  waits=[t for t in wx if t])
                        init = 0.0 if h == 0 else slast_sb[:, an:an + 1]
                        wsc = [tda, tx, buf_cons.get(("s", pb))]
                        if h == 1:
                            wsc.append(buf_cons.get(("slast", an)))
                        tsc = p.op("vector", lambda E, o=s_sb[pb][:, :], da=dA_sb[pb][:, :],
                                   xx=X_sb[pb][:, :], ini=init:
                                   E.tensor_tensor_scan(o, da, xx, ini, OP.mult, OP.add),
                                   waits=[t for t in wsc if t])
                        buf_cons[("dA", pb)] = tsc
                        buf_cons[("X", pb)] = tsc
                        last_tda = tda
                        if h == 1:
                            buf_cons[("slastR", an)] = tsc
                        if h == 0:
                            tsl = p.op("vector", lambda E, o=slast_sb[:, an:an + 1],
                                       i=s_sb[pb][:, TCH - 1:TCH]:
                                       E.tensor_copy(o, i),
                                       waits=[tsc] +
                                       ([buf_cons.get(("slastR", an))] if
                                        buf_cons.get(("slastR", an)) else []))
                            buf_cons[("slast", an)] = tsl
                        wsb = [tsc, tc, buf_cons.get(("sC", pb))]
                        tsb = p.op("vector", lambda E, o=sC_sb[pb][:, :], a=s_sb[pb][:, :],
                                   bq=cbc_sb[pb][:, :]:
                                   E.tensor_tensor(o, a, bq, OP.mult),
                                   waits=[t for t in wsb if t])
                        buf_cons[("s", pb)] = tsb
                        for si, (off, w) in enumerate(spans):
                            wmm = [tsb]
                            if n == 0 and bank_rel.get((et % 2, si)) is not None:
                                wmm.append(bank_rel[(et % 2, si)])
                            fn = (lambda E, ps=bset[si], r=sC_sb[pb][:, off:off + w],
                                  st=(n == 0), sp=(n == NST - 1), w_=w:
                                  E.matmul(ps[:, :w_], ident[:, :], r, start=st, stop=sp))
                            if si == len(spans) - 1:
                                mm_tok = p.op("tensor", fn, waits=wmm)
                            else:
                                p.raw("tensor", fn, waits=wmm)
                        bc_cons[pb] = mm_tok if mm_tok else tsb
                    ye = None
                    for si, (off, w) in enumerate(spans):
                        ye = p.op("vector", lambda E, o=y_sb[et % 2][:, off:off + w],
                                  i=xc_sb[:, et * TCH + off: et * TCH + off + w],
                                  sc=Dv_sb[:, et:et + 1], ps=bset[si], w_=w:
                                  E.scalar_tensor_tensor(o, i, sc, ps[:, :w_],
                                                         OP.mult, OP.add),
                                  waits=[mm_tok, txc] +
                                  ([st_prev.get(et % 2)] if st_prev.get(et % 2) else []))
                        bank_rel[(et % 2, si)] = ye
                        y_evac_toks.append(ye)
                    st_prev[et % 2] = p.dma("sync", s_y[et % 2], oy3[:, et, c0:c0 + TCH],
                                            y_sb[et % 2][:, :],
                                            waits=[ye])
                chunk_cons = [y_evac_toks[-1], last_tda]
                ci += 1

        p.wait_only("gpsimd", (s_y[0], s_y[0].n))
        p.wait_only("gpsimd", (s_y[1], s_y[1].n))
        blk = stack.enter_context(nc.Block())
        p.run(blk)
    return nc


def moe_expert_block(p, ws, hbuf, u_sc, cbc_ap, xn_rhs, spans, banks, bstate,
                     wgT3, wuT3, wdT3, i_mt, d_kt, i_kt, d_mt, macc, macc_init,
                     xn_ready, cbc_tok, u_cons, macc_guard, b1col=None, b2term=None):
    TCW = TC
    g_toks = {}
    for mt in range(i_mt):
        src = (wgT3 if wgT3 is not None else wuT3)
        bufi, wtok = ws.load(src[:, :, mt * 128:(mt + 1) * 128], d_kt)
        mm_last = None
        for si, (off, w) in enumerate(spans):
            b = (mt * len(spans) + si) % 4
            waits = [wtok, xn_ready]
            if bstate.get(b) is not None:
                waits.append(bstate[b])
            for kt in range(d_kt):
                fn = (lambda E, ps=banks[b], wt=ws.tile(bufi, kt), r=xn_rhs(kt, off, w),
                      st=(kt == 0), sp=(kt == d_kt - 1), w_=w:
                      E.matmul(ps[:, :w_], wt, r, start=st, stop=sp))
                if kt == d_kt - 1:
                    mm_last = p.op("tensor", fn, waits=waits)
                else:
                    p.raw("tensor", fn, waits=waits)
                waits = []
            tok = p.op("scalar", lambda E, o=hbuf[:, mt * TCW + off: mt * TCW + off + w],
                       i=banks[b][:, :w], bc=b1col(mt) if b1col else 0.0:
                       E.activation(o, i, AF.Silu, bias=bc),
                       waits=[mm_last] + ([g_toks[(mt, si)]] if (mt, si) in g_toks else [])
                       + ([macc_guard] if macc_guard else []))
            g_toks[(mt, si)] = tok
            bstate[b] = tok
        ws.consumed(bufi, mm_last)
    h_toks = {}
    if wgT3 is not None:
        for mt in range(i_mt):
            bufi, wtok = ws.load(wuT3[:, :, mt * 128:(mt + 1) * 128], d_kt)
            mm_last = None
            for si, (off, w) in enumerate(spans):
                b = (mt * len(spans) + si) % 4
                waits = [wtok, xn_ready]
                if bstate.get(b) is not None:
                    waits.append(bstate[b])
                for kt in range(d_kt):
                    fn = (lambda E, ps=banks[b], wt=ws.tile(bufi, kt), r=xn_rhs(kt, off, w),
                          st=(kt == 0), sp=(kt == d_kt - 1), w_=w:
                          E.matmul(ps[:, :w_], wt, r, start=st, stop=sp))
                    if kt == d_kt - 1:
                        mm_last = p.op("tensor", fn, waits=waits)
                    else:
                        p.raw("tensor", fn, waits=waits)
                    waits = []
                ue = p.op("scalar", lambda E, o=u_sc[si][:, :w], i=banks[b][:, :w]:
                          E.activation(o, i, AF.Copy),
                          waits=[mm_last] + ([u_cons[si]] if u_cons[si] else []))
                th = p.op("vector", lambda E, o=hbuf[:, mt * TCW + off: mt * TCW + off + w],
                          a=hbuf[:, mt * TCW + off: mt * TCW + off + w], bq=u_sc[si][:, :w]:
                          E.tensor_tensor(o, a, bq, OP.mult),
                          waits=[ue, g_toks[(mt, si)]])
                u_cons[si] = th
                if cbc_ap is not None:
                    th = p.op("vector", lambda E, o=hbuf[:, mt * TCW + off: mt * TCW + off + w],
                              a=hbuf[:, mt * TCW + off: mt * TCW + off + w],
                              bq=cbc_ap[:, off:off + w]:
                              E.tensor_tensor(o, a, bq, OP.mult),
                              waits=[th, cbc_tok] if cbc_tok else [th])
                h_toks[(mt, si)] = th
                bstate[b] = th
            ws.consumed(bufi, mm_last)
    else:
        for mt in range(i_mt):
            for si, (off, w) in enumerate(spans):
                th = g_toks[(mt, si)]
                if cbc_ap is not None:
                    th = p.op("vector", lambda E, o=hbuf[:, mt * TCW + off: mt * TCW + off + w],
                              a=hbuf[:, mt * TCW + off: mt * TCW + off + w],
                              bq=cbc_ap[:, off:off + w]:
                              E.tensor_tensor(o, a, bq, OP.mult),
                              waits=[th, cbc_tok] if cbc_tok else [th])
                h_toks[(mt, si)] = th

    h_ready = [h_toks[(mt, si)] for mt in range(i_mt) for si in range(len(spans))]
    macc_toks = []
    for mtd in range(d_mt):
        bufi, wtok = ws.load(wdT3[:, :, mtd * 128:(mtd + 1) * 128], i_kt)
        mm_last = None
        for si, (off, w) in enumerate(spans):
            b = 4 + ((mtd * len(spans) + si) % 4)
            waits = [wtok] + h_ready
            if bstate.get(b) is not None:
                waits.append(bstate[b])
            for kt in range(i_kt):
                fn = (lambda E, ps=banks[b], wt=ws.tile(bufi, kt),
                      r=hbuf[:, kt * TCW + off: kt * TCW + off + w],
                      st=(kt == 0), sp=(kt == i_kt - 1), w_=w:
                      E.matmul(ps[:, :w_], wt, r, start=st, stop=sp))
                if kt == i_kt - 1:
                    mm_last = p.op("tensor", fn, waits=waits)
                else:
                    p.raw("tensor", fn, waits=waits)
                waits = []
            mv = macc[:, mtd * TCW + off: mtd * TCW + off + w]
            if macc_init:
                tok = p.op("vector", lambda E, o=mv, i=banks[b][:, :w]:
                           E.tensor_copy(o, i),
                           waits=[mm_last] + ([macc_guard] if macc_guard else []))
            else:
                tok = p.op("vector", lambda E, o=mv, i=banks[b][:, :w]:
                           E.tensor_tensor(o, o, i, OP.add), waits=[mm_last])
            macc_toks.append(tok)
            bstate[b] = tok
        ws.consumed(bufi, mm_last)
    return macc_toks, h_ready[-1]


def build_launch_C():
    nc = bass.Bass("TRN2", target_bir_lowering=False)
    KT = D // 128
    ET = E // 128
    IT = IEXP // 128
    SHT = ISH // 128
    SHH = SHT // 2
    y_t = nc.dram_tensor("y_t", [E, TC], BF16, kind="ExternalInput").ap()
    sg = nc.dram_tensor("sg", [E, TC], BF16, kind="ExternalInput").ap()
    xn0 = nc.dram_tensor("xn0", [D, TC], BF16, kind="ExternalInput").ap()
    x0 = nc.dram_tensor("x0", [D, TC], F32, kind="ExternalInput").ap()
    woutT = nc.dram_tensor("woutT", [E, D], F32, kind="ExternalInput").ap()
    gatewT = nc.dram_tensor("gatewT", [D, NEXP], F32, kind="ExternalInput").ap()
    shgT = nc.dram_tensor("shgT", [D, ISH], F32, kind="ExternalInput").ap()
    shuT = nc.dram_tensor("shuT", [D, ISH], F32, kind="ExternalInput").ap()
    shdT = nc.dram_tensor("shdT", [ISH, D], F32, kind="ExternalInput").ap()
    egT = nc.dram_tensor("egT", [NEXP, D, IEXP], F32, kind="ExternalInput").ap()
    euT = nc.dram_tensor("euT", [NEXP, D, IEXP], F32, kind="ExternalInput").ap()
    edT = nc.dram_tensor("edT", [NEXP, IEXP, D], F32, kind="ExternalInput").ap()
    o_x1 = nc.dram_tensor("o_x1", [D, TC], F32, kind="ExternalOutput").ap()
    o_xn1 = nc.dram_tensor("o_xn1", [D, TC], BF16, kind="ExternalOutput").ap()

    spans = _spans(TC)
    nsp = len(spans)

    with ExitStack() as stack:
        p = Prog(nc, stack)
        sb = lambda n, s, d: stack.enter_context(nc.sbuf_tensor(n, s, d))
        s_in = p.sem("s_in")
        s_c = p.sem("s_c")
        p.init_consts([1e-5, 1.0, -1.0], s_c)
        s0_sb = sb("s0_sb", [128, KT * TC], BF16)
        macc = sb("macc", [128, KT * TC], F32)
        ct_sb = sb("ct_sb", [1, NEXP * TC], BF16)
        gatew_sb = sb("gatew_sb", [128, KT * NEXP], BF16)
        ones_col = sb("ones_col", [128, 1], BF16)
        ones_row32 = sb("ones_row32", [1, 128], F32)
        ones_rowbf = sb("ones_rowbf", [1, 128], BF16)
        ident32 = sb("ident32", [128, 128], F32)
        f32_sc = sb("f32_sc", [128, 512], F32)
        banks = [stack.enter_context(nc.psum_tensor(f"pb{i}", [128, 512], F32))
                 for i in range(8)]
        ws = WStream(p, "wc", E)

        co = nc.inline_tensor(np.ones((128, 1), dtype=BF), name="ones_col_c")
        cr = nc.inline_tensor(np.ones((1, 128), dtype=np.float32), name="ones_row_c")
        crb = nc.inline_tensor(np.ones((1, 128), dtype=BF), name="ones_rowbf_c")
        idc32 = nc.inline_tensor(np.eye(128, dtype=np.float32), name="id32_c")
        p.dma("sync", s_in, ones_col[:, :], co.ap())
        p.dma("sync", s_in, ones_row32[:, :], cr.ap())
        p.dma("sync", s_in, ones_rowbf[:, :], crb.ap())
        p.dma("sync", s_in, ident32[:, :], idc32.ap())
        s_gw = p.sem("s_gw")
        p.dma("gpsimd", s_gw, gatew_sb[:, :].rearrange("p (kt e) -> p kt e", kt=KT),
              _kt_view(gatewT))
        loads = (s_in, s_in.n)
        gw_tok = (s_gw, s_gw.n)
        for e in ("tensor", "vector", "scalar", "gpsimd"):
            p.wait_only(e, loads)

        with nc.sbuf_tensor("z_sb", [128, ET * TC], BF16) as z_sb, \
             nc.sbuf_tensor("y_sc", [128, 2 * TC], BF16) as y_sc, \
             nc.sbuf_tensor("sg_sc", [128, 2 * TC], BF16) as sg_sc:
            s_y = [p.sem(f"s_yl{i}") for i in range(2)]
            s_g2 = [p.sem(f"s_gl{i}") for i in range(2)]
            y3 = y_t.rearrange("(et p) t -> p et t", p=128)
            sg3 = sg.rearrange("(et p) t -> p et t", p=128)
            zc = {}
            for et in range(ET):
                pb = et % 2
                wz = [zc[pb]] if pb in zc else []
                ty = p.dma("sync", s_y[pb], y_sc[:, pb * TC:(pb + 1) * TC],
                           y3[:, et, :], waits=wz)
                tg = p.dma("sync", s_g2[pb], sg_sc[:, pb * TC:(pb + 1) * TC],
                           sg3[:, et, :], waits=wz)
                tz = p.op("vector", lambda E, o=z_sb[:, et * TC:(et + 1) * TC],
                          a=y_sc[:, pb * TC:(pb + 1) * TC],
                          bq=sg_sc[:, pb * TC:(pb + 1) * TC]:
                          E.tensor_tensor(o, a, bq, OP.mult), waits=[ty, tg])
                zc[pb] = tz
            z_done = zc[(ET - 1) % 2]

            def rhs_z(kt, off, w):
                return z_sb[:, kt * TC + off: kt * TC + off + w]

            bstate = {}

            def evac_s0(m_idx, si, off, w, ps, mmtok):
                return p.op("scalar",
                            lambda E, o=s0_sb[:, m_idx * TC + off: m_idx * TC + off + w],
                            i=ps[:, :w]: E.activation(o, i, AF.Copy), waits=[mmtok])

            proj_stream(p, ws, _kt_view(woutT), list(range(KT)), spans, rhs_z,
                        banks[:4], bstate, evac_s0, rhs_waits=[z_done, zc[ET % 2]])
            ph1 = p.tokens()

        with nc.sbuf_tensor("xn0_sb", [128, KT * TC], BF16) as xn0_sb, \
             nc.sbuf_tensor("hbuf", [128, IT * TC], BF16) as hbuf, \
             nc.sbuf_tensor("u_sc0", [128, 512], BF16) as u_sc0, \
             nc.sbuf_tensor("u_sc1", [128, 512], BF16) as u_sc1, \
             nc.sbuf_tensor("cbc", [128, TC], F32) as cbc, \
             nc.sbuf_tensor("r_sc", [128, 11 * NEXP], F32) as r_sc:
            p.barrier(ph1)
            s_xn = p.sem("s_xn")
            txn = p.dma("sync", s_xn, xn0_sb[:, :].rearrange("p (kt t) -> p kt t", kt=KT),
                        _kt_view(xn0), waits=ph1)
            xn_tok = (s_xn, s_xn.n)
            u_sc = [u_sc0, u_sc1]

            def rhs_xn(kt, off, w):
                return xn0_sb[:, kt * TC + off: kt * TC + off + w]

            NTCH = TC // 128
            ct_done = None
            bstate = {}
            prev_r = None
            for tch in range(NTCH):
                b = 6 + (tch % 2)
                mmw = [xn_tok, gw_tok] + ph1
                if bstate.get(b) is not None:
                    mmw.append(bstate[b])
                mml = None
                for kt in range(KT):
                    fn = (lambda E, ps=banks[b],
                          lt=xn0_sb[:, kt * TC + tch * 128: kt * TC + (tch + 1) * 128],
                          r=gatew_sb[:, kt * NEXP:(kt + 1) * NEXP],
                          st=(kt == 0), sp=(kt == KT - 1):
                          E.matmul(ps[:, :NEXP], lt, r, start=st, stop=sp))
                    if kt == KT - 1:
                        mml = p.op("tensor", fn, waits=mmw)
                    else:
                        p.raw("tensor", fn, waits=mmw)
                    mmw = []
                lg = r_sc[:, 0:NEXP]
                nm = r_sc[:, NEXP:NEXP + 1]
                ex = r_sc[:, 2 * NEXP:3 * NEXP]
                sm = r_sc[:, 3 * NEXP:3 * NEXP + 1]
                pr = r_sc[:, 4 * NEXP:5 * NEXP]
                m1 = r_sc[:, 5 * NEXP:5 * NEXP + 1]
                k1 = r_sc[:, 6 * NEXP:7 * NEXP]
                p2 = r_sc[:, 7 * NEXP:8 * NEXP]
                m2 = r_sc[:, 8 * NEXP:8 * NEXP + 1]
                k2 = r_sc[:, 9 * NEXP:10 * NEXP]
                wv = r_sc[:, 10 * NEXP:10 * NEXP + 1]
                cc = r_sc[:, 10 * NEXP + 1:11 * NEXP + 1] if False else r_sc[:, NEXP + 1:2 * NEXP + 1]
                war = [prev_r] if prev_r else []
                t0 = p.op("vector", lambda E, o=lg, i=banks[b][:, :NEXP]:
                          E.tensor_copy(o, i), waits=[mml] + war)
                bstate[b] = t0
                t1 = p.op("vector", lambda E: E.tensor_reduce(nm, lg, AX.X, OP.max, negate=True),
                          waits=[t0])
                t2 = p.op("scalar", lambda E: E.activation(ex, lg, AF.Exp, bias=nm),
                          waits=[t1] + war)
                t3 = p.op("vector", lambda E: E.reduce_sum(sm, ex, AX.X), waits=[t2])
                t4 = p.op("vector", lambda E: E.reciprocal(sm, sm), waits=[t3])
                t5 = p.op("vector", lambda E: E.tensor_scalar(pr, ex, sm, None, OP.mult),
                          waits=[t4])
                t6 = p.op("vector", lambda E: E.tensor_reduce(m1, pr, AX.X, OP.max),
                          waits=[t5])
                t7 = p.op("vector", lambda E: E.tensor_single_scalar(k1, pr, m1, OP.is_equal),
                          waits=[t6])
                t8 = p.op("vector", lambda E: E.scalar_tensor_tensor(p2, k1, -1e9, pr, OP.mult, OP.add),
                          waits=[t7])
                t9 = p.op("vector", lambda E: E.tensor_reduce(m2, p2, AX.X, OP.max),
                          waits=[t8])
                t10 = p.op("vector", lambda E: E.tensor_single_scalar(k2, p2, m2, OP.is_equal),
                           waits=[t9])
                t11 = p.op("vector", lambda E: E.tensor_add(wv, m1, m2), waits=[t10])
                t12 = p.op("vector", lambda E: E.tensor_scalar(wv, wv, 1e-20, None, OP.add),
                           waits=[t11])
                t13 = p.op("vector", lambda E: E.reciprocal(wv, wv), waits=[t12])
                t14 = p.op("vector", lambda E: E.tensor_add(cc, k1, k2), waits=[t13])
                t15 = p.op("vector", lambda E: E.tensor_mul(cc, cc, pr), waits=[t14])
                t16 = p.op("vector", lambda E: E.tensor_scalar(cc, cc, wv, None, OP.mult),
                           waits=[t15])
                prev_r = t16
                for ex_i in range(NEXP):
                    b2 = 4 + (ex_i % 2)
                    mw = [t16] + ([bstate.get(b2)] if bstate.get(b2) else [])
                    tt = p.op("tensor", lambda E, ps=banks[b2], col=cc[:, ex_i:ex_i + 1]:
                              E.matmul(ps[0:1, 0:128], col, ident32[:, :],
                                       start=True, stop=True), waits=mw)
                    te = p.op("vector", lambda E,
                              o=ct_sb[0:1, ex_i * TC + tch * 128: ex_i * TC + (tch + 1) * 128],
                              i=banks[b2][0:1, 0:128]: E.tensor_copy(o, i), waits=[tt])
                    bstate[b2] = te
                    ct_done = te
                    prev_r = te

            u_cons = [None] * nsp
            bstate = {}
            shg3 = _kt_view(shgT)
            shu3 = _kt_view(shuT)
            shd3 = _kt_view(shdT)
            for half in range(2):
                macc_toks, _ = moe_expert_block(
                    p, ws, hbuf, u_sc, None, rhs_xn, spans, banks, bstate,
                    shg3[:, :, half * IEXP:(half + 1) * IEXP],
                    shu3[:, :, half * IEXP:(half + 1) * IEXP],
                    shd3[:, half * IT:(half + 1) * IT, :],
                    IT, KT, IT, KT, macc, half == 0, xn_tok, None, u_cons,
                    macc_guard=None)
            cbc_tok = None
            for ex_i in range(NEXP):
                cb_toks = []
                for si, (off, w) in enumerate(spans):
                    b = 6 + (si % 2)
                    mmw = [ct_done] + ([bstate.get(b)] if bstate.get(b) else [])
                    tt = p.op("tensor", lambda E, ps=banks[b],
                              r=ct_sb[0:1, ex_i * TC + off: ex_i * TC + off + w],
                              w_=w: E.matmul(ps[:, :w_], ones_rowbf[:, :], r,
                                             start=True, stop=True),
                              waits=mmw)
                    tev = p.op("scalar", lambda E, o=cbc[:, off:off + w], i=banks[b][:, :w]:
                               E.activation(o, i, AF.Copy),
                               waits=[tt] + ([cbc_tok] if cbc_tok else []))
                    bstate[b] = tev
                    cb_toks.append(tev)
                macc_toks, cbc_tok = moe_expert_block(
                    p, ws, hbuf, u_sc, cbc[:, :], rhs_xn, spans, banks, bstate,
                    _kt_view(egT[ex_i]), _kt_view(euT[ex_i]), _kt_view(edT[ex_i]),
                    IT, KT, IT, KT, macc, False, xn_tok,
                    cb_toks[-1], u_cons, macc_guard=None)
            g0_toks = []
            gprev = None
            for kt in range(KT):
                gprev = p.op("scalar", lambda E, o=macc[:, kt * TC:(kt + 1) * TC],
                             i=macc[:, kt * TC:(kt + 1) * TC]:
                             E.activation(o, i, AF.Silu),
                             waits=macc_toks + ([gprev] if gprev else []))
                g0_toks.append(gprev)
            ph2 = p.tokens()

        with nc.sbuf_tensor("x1b", [128, KT * TC], BF16) as x1b, \
             nc.sbuf_tensor("sq_sc", [128, TC], BF16) as sq_sc, \
             nc.sbuf_tensor("rows", [1, 6 * TC], F32) as rows, \
             nc.sbuf_tensor("x0_sc", [128, 2 * TC], F32) as x0_sc, \
             nc.sbuf_tensor("x1_sc", [128, 2 * TC], F32) as x1_sc:
            p.barrier(ph2)
            s_x0 = [p.sem(f"s_x0{i}") for i in range(2)]
            s_x1 = [p.sem(f"s_x1{i}") for i in range(2)]
            s_st = p.sem("s_st")
            consts = {"ones_col": ones_col[:, :], "ones_row32": ones_row32[:, :],
                      "f32_sc": f32_sc}
            prevx = {}
            x1_done = []
            for kt in range(KT):
                pb = kt % 2
                wx = prevx.get(pb, [])
                tx0 = p.dma("sync", s_x0[pb], x0_sc[:, pb * TC:(pb + 1) * TC],
                            _kt_view(x0)[:, kt, :], waits=wx)
                tm = p.op("vector", lambda E, o=x1_sc[:, pb * TC:(pb + 1) * TC],
                          a=s0_sb[:, kt * TC:(kt + 1) * TC],
                          bq=macc[:, kt * TC:(kt + 1) * TC]:
                          E.tensor_tensor(o, a, bq, OP.mult),
                          waits=[g0_toks[kt]] + wx)
                ta = p.op("vector", lambda E, o=x1_sc[:, pb * TC:(pb + 1) * TC],
                          a=x1_sc[:, pb * TC:(pb + 1) * TC],
                          bq=x0_sc[:, pb * TC:(pb + 1) * TC]:
                          E.tensor_tensor(o, a, bq, OP.add), waits=[tm, tx0])
                tst = p.dma("sync", s_x1[pb], _kt_view(o_x1)[:, kt, :],
                            x1_sc[:, pb * TC:(pb + 1) * TC], waits=[ta])
                tb = p.op("scalar", lambda E, o=x1b[:, kt * TC:(kt + 1) * TC],
                          i=x1_sc[:, pb * TC:(pb + 1) * TC]:
                          E.activation(o, i, AF.Copy), waits=[ta])
                prevx[pb] = [tst, tb]
                x1_done.append(tb)
            xn1_done = layernorm_fm(p, x1b, x1b, sq_sc, rows, banks, consts, TC, KT,
                                    x_ready=x1_done[-1], bc_banks=(6, 7))
            p.dma("sync", s_st, o_xn1.rearrange("(kt p) t -> p kt t", p=128),
                  x1b[:, :].rearrange("p (kt t) -> p kt t", kt=KT),
                  waits=[xn1_done])
            p.wait_only("gpsimd", (s_st, s_st.n))
            p.wait_only("gpsimd", (s_x1[0], s_x1[0].n))
            p.wait_only("gpsimd", (s_x1[1], s_x1[1].n))

        blk = stack.enter_context(nc.Block())
        p.run(blk)
    return nc


def build_launch_D():
    nc = bass.Bass("TRN2", target_bir_lowering=False)
    NH = 4
    KTT = SEQ // 128
    NQC = SEQ // 512
    KT = D // 128
    SCALE = 1.0 / float(np.sqrt(HDIM))

    xn1b = nc.dram_tensor("xn1b", [D, SEQ], BF16, kind="ExternalInput").ap()
    wqkvT = nc.dram_tensor("wqkvT", [128, 12 * (D // 128) * 128], F32,
                           kind="ExternalInput").ap()
    bqkv = nc.dram_tensor("bqkv", [128, 12], F32, kind="ExternalInput").ap()
    alibi = nc.dram_tensor("alibi", [NH, SEQ], F32, kind="ExternalInput").ap()
    alibi_colT = nc.dram_tensor("alibi_colT", [128, NH * KTT], F32,
                                kind="ExternalInput").ap()
    o_ctx = nc.dram_tensor("o_ctx", [NH * HDIM, SEQ], BF16, kind="ExternalOutput").ap()

    with ExitStack() as stack:
        p = Prog(nc, stack)
        sb = lambda n, s, d: stack.enter_context(nc.sbuf_tensor(n, s, d))
        s_in = p.sem("s_in")
        s_c = p.sem("s_c")
        p.init_consts([1e-5, 1.0, -1.0, SCALE], s_c)
        q_sb = sb("q_sb", [128, NH * SEQ], BF16)
        k_sb = sb("k_sb", [128, NH * SEQ], BF16)
        v_sb = sb("v_sb", [128, KTT * NH * 128], BF16)
        acolT = sb("acolT", [128, NH * KTT], F32)
        ident = sb("ident", [128, 128], BF16)
        ident32 = sb("ident32", [128, 128], F32)
        ones_row32 = sb("ones_row32", [1, 128], F32)
        banks = [stack.enter_context(nc.psum_tensor(f"pb{i}", [128, 512], F32))
                 for i in range(6)]
        banksb = [stack.enter_context(nc.psum_tensor(f"pt{i}", [128, 128], BF16))
                  for i in range(2)]

        idc = nc.inline_tensor(np.eye(128, dtype=BF), name="ident_c")
        idc32 = nc.inline_tensor(np.eye(128, dtype=np.float32), name="id32_c")
        cr = nc.inline_tensor(np.ones((1, 128), dtype=np.float32), name="ones_row_c")
        p.dma("sync", s_in, ident[:, :], idc.ap())
        p.dma("sync", s_in, ident32[:, :], idc32.ap())
        p.dma("sync", s_in, ones_row32[:, :], cr.ap())
        p.dma("sync", s_in, acolT[:, :], alibi_colT)
        loads = (s_in, s_in.n)
        for e in ("tensor", "vector", "scalar", "gpsimd"):
            p.wait_only(e, loads)

        with nc.sbuf_tensor("xn_sc", [128, 2 * KT * 512], BF16) as xn_sc, \
             nc.sbuf_tensor("wq_sb", [128, 12 * KT * 128], BF16) as wq_sb, \
             nc.sbuf_tensor("bq_sb", [128, 12], F32) as bq_sb, \
             nc.sbuf_tensor("v_sc", [128, 512], BF16) as v_sc:
            s_w = p.sem("s_w")
            s_b = p.sem("s_b")
            tw = p.dma("gpsimd", s_w, wq_sb[:, :], wqkvT)
            tb = p.dma("sync", s_b, bq_sb[:, :], bqkv)
            s_xn = [p.sem(f"s_xn{i}") for i in range(2)]
            xn3 = _kt_view(xn1b)
            xn_cons = {}
            vp_toks = {}
            bstate = {}
            vtrans_prev = {}
            for nch in range(NQC):
                pb = nch % 2
                wx = xn_cons.get(pb, [])
                txn = p.dma("sync", s_xn[pb],
                            xn_sc[:, pb * KT * 512:(pb + 1) * KT * 512]
                            .rearrange("p (kt t) -> p kt t", kt=KT),
                            xn3[:, :, nch * 512:(nch + 1) * 512], waits=wx)
                last_mm = None
                for m in range(12):
                    b = (nch * 12 + m) % 4
                    mmw = [txn, tw]
                    if bstate.get(b) is not None:
                        mmw.append(bstate[b])
                    mml = None
                    for kt in range(KT):
                        fn = (lambda E, ps=banks[b],
                              wt=wq_sb[:, (m * KT + kt) * 128:(m * KT + kt + 1) * 128],
                              r=xn_sc[:, pb * KT * 512 + kt * 512:(pb * KT + kt) * 512 + 512],
                              st=(kt == 0), sp=(kt == KT - 1):
                              E.matmul(ps[:, :], wt, r, start=st, stop=sp))
                        if kt == KT - 1:
                            mml = p.op("tensor", fn, waits=mmw)
                        else:
                            p.raw("tensor", fn, waits=mmw)
                        mmw = []
                    last_mm = mml
                    if m < 8:
                        dst = q_sb if m < 4 else k_sb
                        h = m % 4
                        tok = p.op("scalar", lambda E,
                                   o=dst[:, h * SEQ + nch * 512: h * SEQ + nch * 512 + 512],
                                   i=banks[b][:, :], bc=bq_sb[:, m:m + 1]:
                                   E.activation(o, i, AF.Identity, bias=bc),
                                   waits=[mml, tb])
                        bstate[b] = tok
                    else:
                        h = m - 8
                        tok = p.op("scalar", lambda E, o=v_sc[:, :],
                                   i=banks[b][:, :], bc=bq_sb[:, m:m + 1]:
                                   E.activation(o, i, AF.Identity, bias=bc),
                                   waits=[mml, tb] +
                                   ([vp_toks[("sc",)]] if ("sc",) in vp_toks else []))
                        for sub in range(4):
                            kt_tok = nch * 4 + sub
                            b2 = sub % 2
                            tt = p.op("tensor", lambda E, o=banksb[b2][:, :],
                                      i=v_sc[:, sub * 128:(sub + 1) * 128]:
                                      E.transpose(o, i, ident[:, :]),
                                      waits=[tok] +
                                      ([vtrans_prev[b2]] if b2 in vtrans_prev else []))
                            tv = p.op("vector", lambda E,
                                      o=v_sb[:, kt_tok * NH * 128 + h * 128:
                                             kt_tok * NH * 128 + (h + 1) * 128],
                                      i=banksb[b2][:, :]: E.tensor_copy(o, i),
                                      waits=[tt])
                            vtrans_prev[b2] = tv
                            vp_toks[(kt_tok, h)] = tv
                        vp_toks[("sc",)] = tv
                        bstate[b] = tv
                xn_cons[pb] = [last_mm]
            qkv_done = p.tokens()

        with nc.sbuf_tensor("abc", [128, SEQ], F32) as abc, \
             nc.sbuf_tensor("mbc", [128, SEQ], F32) as mbc, \
             nc.sbuf_tensor("rbc", [128, SEQ], F32) as rbc, \
             nc.sbuf_tensor("mrow", [1, SEQ], F32) as mrow, \
             nc.sbuf_tensor("rrow", [1, SEQ], F32) as rrow, \
             nc.sbuf_tensor("lsc", [128, 2 * 512], F32) as lsc, \
             nc.sbuf_tensor("lrow", [128, SEQ], F32) as lrow, \
             nc.sbuf_tensor("stats", [128, 4 + 2 * KTT], F32) as stats, \
             nc.sbuf_tensor("psc", [128, 2 * 512], BF16) as psc, \
             nc.sbuf_tensor("osc", [128, 2 * 512], BF16) as osc:
            p.barrier(qkv_done)
            s_ab = p.sem("s_ab")
            s_os = [p.sem(f"s_os{i}") for i in range(2)]
            mneg_col = stats[:, 4:4 + KTT]
            rrec_col = stats[:, 4 + KTT:4 + 2 * KTT]
            fillc = p.pool_reg(-1e9)
            p1rel = {}
            p2rel = {}
            p2psc = {}
            rowrel = {}
            orel = {}
            ost = {}
            abc_tok = None
            for h in range(NH):
                tab = p.dma("gpsimd", s_ab, abc[:, :],
                            alibi[h:h + 1, :].broadcast_to([128, SEQ]),
                            waits=p.tokens() if h > 0 else [])
                p1stat = {}
                rs_guard = None
                for qt in range(KTT):
                    nkc = qt // 4 + 1
                    row_toks = []
                    for kc in range(nkc):
                        b = kc % 2
                        tS = p.op("tensor", lambda E, ps=banks[b],
                                  lt=q_sb[:, h * SEQ + qt * 128: h * SEQ + (qt + 1) * 128],
                                  r=k_sb[:, h * SEQ + kc * 512: h * SEQ + (kc + 1) * 512]:
                                  E.matmul(ps[:, :], lt, r, start=True, stop=True),
                                  waits=[p1rel[b]] if b in p1rel else [])
                        tL = p.op("vector", lambda E, o=lrow[:, kc * 512:(kc + 1) * 512],
                                  i=banks[b][:, :], a=abc[:, kc * 512:(kc + 1) * 512]:
                                  E.scalar_tensor_tensor(o, i, SCALE, a, OP.mult, OP.add),
                                  waits=[tS, tab] + ([rs_guard] if rs_guard else []))
                        p1rel[b] = tL
                        if kc == nkc - 1:
                            tL = p.op("gpsimd", lambda E, o=lrow[:, kc * 512:(kc + 1) * 512],
                                      i=lrow[:, kc * 512:(kc + 1) * 512],
                                      base=qt * 128 - kc * 512:
                                      E.affine_select(o, i, [[-1, 512]], OP.is_ge,
                                                      fillc["r"], base=base,
                                                      channel_multiplier=1),
                                      waits=[tL])
                        row_toks.append(tL)
                    tmx = p.op("vector", lambda E, o=mneg_col[:, qt:qt + 1],
                               i=lrow[:, 0:nkc * 512]:
                               E.tensor_reduce(o, i, AX.X, OP.max, negate=True),
                               waits=row_toks)
                    rs_guard = p.op("scalar", lambda E, o=lrow[:, 0:nkc * 512],
                                    i=lrow[:, 0:nkc * 512], qt=qt:
                                    E.activation(o, i, AF.Exp, bias=mneg_col[:, qt:qt + 1],
                                                 accum_out=rrec_col[:, qt:qt + 1]),
                                    waits=[tmx] + ([rs_guard] if rs_guard else []))
                    p1stat[qt] = rs_guard
                trc = p.op("vector", lambda E: E.reciprocal(rrec_col[:, :], rrec_col[:, :]),
                           waits=[rs_guard])
                for qt in range(KTT):
                    p1stat[qt] = trc
                rowt = None
                for qt in range(KTT):
                    bm = 4
                    br = 5
                    mw = [p1stat[qt]]
                    if rowrel.get(bm) is not None:
                        mw.append(rowrel[bm])
                    t_a = p.op("tensor", lambda E, ps=banks[bm], col=mneg_col[:, qt:qt + 1]:
                               E.matmul(ps[0:1, 0:128], col, ident32[:, :],
                                        start=True, stop=True), waits=mw)
                    mw2 = [p1stat[qt]]
                    if rowrel.get(br) is not None:
                        mw2.append(rowrel[br])
                    t_b = p.op("tensor", lambda E, ps=banks[br], col=rrec_col[:, qt:qt + 1]:
                               E.matmul(ps[0:1, 0:128], col, ident32[:, :],
                                        start=True, stop=True), waits=mw2)
                    tev = p.op("vector", lambda E, o=mrow[:, qt * 128:(qt + 1) * 128],
                               i=banks[bm][0:1, 0:128]: E.tensor_copy(o, i), waits=[t_a])
                    rowrel[bm] = tev
                    tev2 = p.op("vector", lambda E, o=rrow[:, qt * 128:(qt + 1) * 128],
                                i=banks[br][0:1, 0:128]: E.tensor_copy(o, i), waits=[t_b])
                    rowrel[br] = tev2
                    rowt = tev2
                bc_done = None
                for si in range(SEQ // 512):
                    bm = 4
                    br = 5
                    tt1 = p.op("tensor", lambda E, ps=banks[bm],
                               r=mrow[:, si * 512:(si + 1) * 512]:
                               E.matmul(ps[:, :], ones_row32[:, :], r, start=True,
                                        stop=True), waits=[rowt, rowrel[bm]])
                    tb1 = p.op("vector", lambda E, o=mbc[:, si * 512:(si + 1) * 512],
                               i=banks[bm][:, :]: E.tensor_copy(o, i), waits=[tt1])
                    rowrel[bm] = tb1
                    tt2 = p.op("tensor", lambda E, ps=banks[br],
                               r=rrow[:, si * 512:(si + 1) * 512]:
                               E.matmul(ps[:, :], ones_row32[:, :], r, start=True,
                                        stop=True), waits=[rowt, rowrel[br]])
                    tb2 = p.op("vector", lambda E, o=rbc[:, si * 512:(si + 1) * 512],
                               i=banks[br][:, :]: E.tensor_copy(o, i), waits=[tt2])
                    rowrel[br] = tb2
                    bc_done = tb2
                for qc in range(NQC):
                    nkt = (qc + 1) * 4
                    ob = 2 + (qc % 2)
                    mml = None
                    for kt in range(nkt):
                        b = kt % 2
                        tS = p.op("tensor", lambda E, ps=banks[b],
                                  lt=k_sb[:, h * SEQ + kt * 128: h * SEQ + (kt + 1) * 128],
                                  r=q_sb[:, h * SEQ + qc * 512: h * SEQ + (qc + 1) * 512]:
                                  E.matmul(ps[:, :], lt, r, start=True, stop=True),
                                  waits=[p2rel[b]] if b in p2rel else [])
                        tT = p.op("vector", lambda E, o=lsc[:, b * 512:(b + 1) * 512],
                                  i=banks[b][:, :], a=mbc[:, qc * 512:(qc + 1) * 512]:
                                  E.scalar_tensor_tensor(o, i, SCALE, a, OP.mult, OP.add),
                                  waits=[tS, bc_done] + ([p2rel[b]] if b in p2rel else []))
                        if kt >= qc * 4:
                            tT = p.op("gpsimd", lambda E, o=lsc[:, b * 512:(b + 1) * 512],
                                      i=lsc[:, b * 512:(b + 1) * 512],
                                      base=qc * 512 - kt * 128:
                                      E.affine_select(o, i, [[1, 512]], OP.is_ge,
                                                      fillc["r"], base=base,
                                                      channel_multiplier=-1),
                                      waits=[tT])
                        tE = p.op("scalar", lambda E, o=psc[:, b * 512:(b + 1) * 512],
                                  i=lsc[:, b * 512:(b + 1) * 512],
                                  bc=acolT[:, h * KTT + kt: h * KTT + kt + 1]:
                                  E.activation(o, i, AF.Exp, bias=bc),
                                  waits=[tT] + ([p2psc[b]] if b in p2psc else []))
                        p2rel[b] = tE
                        vw = [tE]
                        if kt == 0 and orel.get(ob) is not None:
                            vw.append(orel[ob])
                        tPV = p.op("tensor", lambda E, ps=banks[ob],
                                   lt=v_sb[:, kt * NH * 128 + h * 128:
                                           kt * NH * 128 + (h + 1) * 128],
                                   r=psc[:, b * 512:(b + 1) * 512],
                                   st=(kt == 0), sp=(kt == nkt - 1):
                                   E.matmul(ps[:, :], lt, r, start=st, stop=sp),
                                   waits=vw)
                        p2psc[b] = tPV
                        mml = tPV
                    tev = p.op("vector", lambda E, o=osc[:, (qc % 2) * 512:(qc % 2) * 512 + 512],
                               i=banks[ob][:, :], a=rbc[:, qc * 512:(qc + 1) * 512]:
                               E.tensor_tensor(o, i, a, OP.mult),
                               waits=[mml] + ([ost.get(qc % 2)] if ost.get(qc % 2) else []))
                    orel[ob] = tev
                    tst = p.dma("sync", s_os[qc % 2],
                                o_ctx[h * HDIM:(h + 1) * HDIM, qc * 512:(qc + 1) * 512],
                                osc[:, (qc % 2) * 512:(qc % 2) * 512 + 512], waits=[tev])
                    ost[qc % 2] = tst
            p.wait_only("gpsimd", (s_os[0], s_os[0].n))
            p.wait_only("gpsimd", (s_os[1], s_os[1].n))

        blk = stack.enter_context(nc.Block())
        p.run(blk)
    return nc


def build_launch_E():
    nc = bass.Bass("TRN2", target_bir_lowering=False)
    KT = D // 128
    IT = IEXP // 128
    ctx = nc.dram_tensor("ctx", [D, TC], BF16, kind="ExternalInput").ap()
    x1 = nc.dram_tensor("x1", [D, TC], F32, kind="ExternalInput").ap()
    xn1 = nc.dram_tensor("xn1", [D, TC], BF16, kind="ExternalInput").ap()
    woutT = nc.dram_tensor("woutT", [D, D], F32, kind="ExternalInput").ap()
    bout = nc.dram_tensor("bout", [128, KT], F32, kind="ExternalInput").ap()
    gatewT = nc.dram_tensor("gatewT", [D, NEXP], F32, kind="ExternalInput").ap()
    gateb = nc.dram_tensor("gateb", [1, NEXP], F32, kind="ExternalInput").ap()
    ebias = nc.dram_tensor("ebias", [1, NEXP], F32, kind="ExternalInput").ap()
    w1T = nc.dram_tensor("w1T", [NEXP, D, IEXP], F32, kind="ExternalInput").ap()
    b1 = nc.dram_tensor("b1", [128, NEXP * IT], F32, kind="ExternalInput").ap()
    w2T = nc.dram_tensor("w2T", [NEXP, IEXP, D], F32, kind="ExternalInput").ap()
    b2 = nc.dram_tensor("b2", [NEXP, D], F32, kind="ExternalInput").ap()
    o_x2 = nc.dram_tensor("o_x2", [D, TC], F32, kind="ExternalOutput").ap()

    spans = _spans(TC)
    nsp = len(spans)

    with ExitStack() as stack:
        p = Prog(nc, stack)
        sb = lambda n, s, d: stack.enter_context(nc.sbuf_tensor(n, s, d))
        s_in = p.sem("s_in")
        s_c = p.sem("s_c")
        p.init_consts([1e-5, 1.0, -1.0], s_c)
        s1_sb = sb("s1_sb", [128, KT * TC], BF16)
        macc = sb("macc", [128, KT * TC], F32)
        ct_sb = sb("ct_sb", [1, NEXP * TC], BF16)
        c8_sb = sb("c8_sb", [8, TC], BF16)
        gatew_sb = sb("gatew_sb", [128, KT * NEXP], BF16)
        bout_sb = sb("bout_sb", [128, KT], F32)
        b1_sb = sb("b1_sb", [128, NEXP * IT], F32)
        b2_sb = sb("b2_sb", [8, D], BF16)
        gbbc = sb("gbbc", [128, NEXP], F32)
        grow = sb("grow", [1, 2 * NEXP], F32)
        ones_col = sb("ones_col", [128, 1], BF16)
        ones_row32 = sb("ones_row32", [1, 128], F32)
        ones_rowbf = sb("ones_rowbf", [1, 128], BF16)
        ident32 = sb("ident32", [128, 128], F32)
        f32_sc = sb("f32_sc", [128, 512], F32)
        banks = [stack.enter_context(nc.psum_tensor(f"pb{i}", [128, 512], F32))
                 for i in range(8)]
        ws = WStream(p, "we", D)

        co = nc.inline_tensor(np.ones((128, 1), dtype=BF), name="ones_col_c")
        cr = nc.inline_tensor(np.ones((1, 128), dtype=np.float32), name="ones_row_c")
        crb = nc.inline_tensor(np.ones((1, 128), dtype=BF), name="ones_rowbf_c")
        idc32 = nc.inline_tensor(np.eye(128, dtype=np.float32), name="id32_c")
        p.dma("sync", s_in, ones_col[:, :], co.ap())
        p.dma("sync", s_in, ones_row32[:, :], cr.ap())
        p.dma("sync", s_in, ones_rowbf[:, :], crb.ap())
        p.dma("sync", s_in, ident32[:, :], idc32.ap())
        p.dma("sync", s_in, bout_sb[:, :], bout)
        p.dma("sync", s_in, b1_sb[:, :], b1)
        p.dma("sync", s_in, grow[:, 0:NEXP], gateb)
        p.dma("sync", s_in, grow[:, NEXP:2 * NEXP], ebias)
        s_gw = p.sem("s_gw")
        p.dma("gpsimd", s_gw, gatew_sb[:, :].rearrange("p (kt e) -> p kt e", kt=KT),
              _kt_view(gatewT))
        p.dma("gpsimd", s_gw, b2_sb[:, :], b2)
        loads = (s_in, s_in.n)
        gw_tok = (s_gw, s_gw.n)
        for e in ("tensor", "vector", "scalar", "gpsimd"):
            p.wait_only(e, loads)
            p.wait_only(e, gw_tok)

        t0 = p.op("vector", lambda E: E.tensor_add(grow[:, 0:NEXP], grow[:, 0:NEXP],
                                                   grow[:, NEXP:2 * NEXP]),
                  waits=[loads])
        t1 = p.op("tensor", lambda E: E.matmul(banks[7][:, :NEXP], ones_row32[:, :],
                                               grow[:, 0:NEXP], start=True, stop=True),
                  waits=[t0])
        gb_tok = p.op("vector", lambda E: E.tensor_copy(gbbc[:, :], banks[7][:, :NEXP]),
                      waits=[t1])

        with nc.sbuf_tensor("ctx_sb", [128, KT * TC], BF16) as ctx_sb:
            s_cx = p.sem("s_cx")
            tcx = p.dma("sync", s_cx, ctx_sb[:, :].rearrange("p (kt t) -> p kt t", kt=KT),
                        _kt_view(ctx))
            cx_tok = (s_cx, s_cx.n)

            def rhs_cx(kt, off, w):
                return ctx_sb[:, kt * TC + off: kt * TC + off + w]

            bstate = {}

            def evac_s1(m_idx, si, off, w, ps, mmtok):
                return p.op("scalar",
                            lambda E, o=s1_sb[:, m_idx * TC + off: m_idx * TC + off + w],
                            i=ps[:, :w], bc=bout_sb[:, m_idx:m_idx + 1]:
                            E.activation(o, i, AF.Identity, bias=bc), waits=[mmtok])

            proj_stream(p, ws, _kt_view(woutT), list(range(KT)), spans, rhs_cx,
                        banks[:4], bstate, evac_s1, rhs_waits=[cx_tok])
            ph1 = p.tokens()

        with nc.sbuf_tensor("xn1_sb", [128, KT * TC], BF16) as xn1_sb, \
             nc.sbuf_tensor("hbuf", [128, IT * TC], BF16) as hbuf, \
             nc.sbuf_tensor("u_sc0", [128, 512], BF16) as u_sc0, \
             nc.sbuf_tensor("u_sc1", [128, 512], BF16) as u_sc1, \
             nc.sbuf_tensor("cbc", [128, TC], F32) as cbc, \
             nc.sbuf_tensor("r_sc", [128, 11 * NEXP], F32) as r_sc:
            p.barrier(ph1)
            s_xn = p.sem("s_xn")
            txn = p.dma("sync", s_xn, xn1_sb[:, :].rearrange("p (kt t) -> p kt t", kt=KT),
                        _kt_view(xn1), waits=ph1)
            xn_tok = (s_xn, s_xn.n)
            u_sc = [u_sc0, u_sc1]

            def rhs_xn(kt, off, w):
                return xn1_sb[:, kt * TC + off: kt * TC + off + w]

            NTCH = TC // 128
            ct_done = None
            c8_done = None
            bstate = {}
            prev_r = None
            for tch in range(NTCH):
                b = 6 + (tch % 2)
                mmw = [xn_tok, gw_tok, gb_tok]
                if bstate.get(b) is not None:
                    mmw.append(bstate[b])
                mml = None
                for kt in range(KT):
                    fn = (lambda E, ps=banks[b],
                          lt=xn1_sb[:, kt * TC + tch * 128: kt * TC + (tch + 1) * 128],
                          r=gatew_sb[:, kt * NEXP:(kt + 1) * NEXP],
                          st=(kt == 0), sp=(kt == KT - 1):
                          E.matmul(ps[:, :NEXP], lt, r, start=st, stop=sp))
                    if kt == KT - 1:
                        mml = p.op("tensor", fn, waits=mmw)
                    else:
                        p.raw("tensor", fn, waits=mmw)
                    mmw = []
                lg = r_sc[:, 0:NEXP]
                m1 = r_sc[:, NEXP:NEXP + 1]
                k1 = r_sc[:, 2 * NEXP:3 * NEXP]
                p2 = r_sc[:, 3 * NEXP:4 * NEXP]
                m2 = r_sc[:, 4 * NEXP:4 * NEXP + 1]
                k2 = r_sc[:, 5 * NEXP:6 * NEXP]
                g1v = r_sc[:, 6 * NEXP:6 * NEXP + 1]
                g2v = r_sc[:, 6 * NEXP + 1:6 * NEXP + 2]
                dd = r_sc[:, 6 * NEXP + 2:6 * NEXP + 3]
                cc = r_sc[:, 7 * NEXP:8 * NEXP]
                war = [prev_r] if prev_r else []
                t0 = p.op("vector", lambda E, o=lg, i=banks[b][:, :NEXP]:
                          E.tensor_tensor(o, i, gbbc[:, :], OP.add),
                          waits=[mml, gb_tok] + war)
                bstate[b] = t0
                t1 = p.op("vector", lambda E: E.tensor_reduce(m1, lg, AX.X, OP.max),
                          waits=[t0])
                t2 = p.op("vector", lambda E: E.tensor_single_scalar(k1, lg, m1, OP.is_equal),
                          waits=[t1])
                t3 = p.op("vector", lambda E: E.scalar_tensor_tensor(p2, k1, -1e9, lg, OP.mult, OP.add),
                          waits=[t2])
                t4 = p.op("vector", lambda E: E.tensor_reduce(m2, p2, AX.X, OP.max),
                          waits=[t3])
                t5 = p.op("vector", lambda E: E.tensor_single_scalar(k2, p2, m2, OP.is_equal),
                          waits=[t4])
                t6 = p.op("vector", lambda E: E.tensor_sub(dd, m1, m2), waits=[t5])
                t7 = p.op("scalar", lambda E: E.activation(g1v, dd, AF.Sigmoid),
                          waits=[t6] + war)
                t8 = p.op("scalar", lambda E: E.activation(g2v, g1v, AF.Identity,
                                                           scale=-1.0, bias=1.0),
                          waits=[t7])
                t9 = p.op("vector", lambda E: E.tensor_scalar(cc, k1, g1v, None, OP.mult),
                          waits=[t8])
                t10 = p.op("vector", lambda E: E.scalar_tensor_tensor(cc, k2, g2v, cc, OP.mult, OP.add),
                           waits=[t9])
                prev_r = t10
                for ex_i in range(NEXP):
                    b2i = 4 + (ex_i % 2)
                    mw = [t10] + ([bstate.get(b2i)] if bstate.get(b2i) else [])
                    tt = p.op("tensor", lambda E, ps=banks[b2i], col=cc[:, ex_i:ex_i + 1]:
                              E.matmul(ps[0:1, 0:128], col, ident32[:, :],
                                       start=True, stop=True), waits=mw)
                    te = p.op("vector", lambda E,
                              o=ct_sb[0:1, ex_i * TC + tch * 128: ex_i * TC + (tch + 1) * 128],
                              i=banks[b2i][0:1, 0:128]: E.tensor_copy(o, i), waits=[tt])
                    bstate[b2i] = te
                    ct_done = te
                    prev_r = te
                b3 = 6 + (tch % 2)
                tt2 = p.op("tensor", lambda E, ps=banks[b3], i=cc:
                           E.transpose(ps[0:NEXP, 0:128], i, ident32[:, :]),
                           waits=[t10, bstate[b3]])
                te2 = p.op("vector", lambda E, o=c8_sb[:, tch * 128:(tch + 1) * 128],
                           i=banks[b3][0:NEXP, 0:128]: E.tensor_copy(o, i), waits=[tt2])
                bstate[b3] = te2
                c8_done = te2
                prev_r = te2

            macc_toks = []
            for mt in range(KT):
                for si, (off, w) in enumerate(spans):
                    b = 4 + ((mt * nsp + si) % 4)
                    mw = [c8_done, gw_tok]
                    if bstate.get(b) is not None:
                        mw.append(bstate[b])
                    tt = p.op("tensor", lambda E, ps=banks[b],
                              lt=b2_sb[0:8, mt * 128:(mt + 1) * 128],
                              r=c8_sb[0:8, off:off + w], w_=w:
                              E.matmul(ps[:, :w_], lt, r, start=True, stop=True),
                              waits=mw)
                    tok = p.op("vector", lambda E, o=macc[:, mt * TC + off: mt * TC + off + w],
                               i=banks[b][:, :w]: E.tensor_copy(o, i), waits=[tt])
                    bstate[b] = tok
                    macc_toks.append(tok)

            u_cons = [None] * nsp
            cbc_tok = None
            for ex_i in range(NEXP):
                cb_toks = []
                for si, (off, w) in enumerate(spans):
                    b = 6 + (si % 2)
                    mmw = [ct_done] + ([bstate.get(b)] if bstate.get(b) else [])
                    tt = p.op("tensor", lambda E, ps=banks[b],
                              r=ct_sb[0:1, ex_i * TC + off: ex_i * TC + off + w],
                              w_=w: E.matmul(ps[:, :w_], ones_rowbf[:, :], r,
                                             start=True, stop=True),
                              waits=mmw)
                    tev = p.op("scalar", lambda E, o=cbc[:, off:off + w], i=banks[b][:, :w]:
                               E.activation(o, i, AF.Copy),
                               waits=[tt] + ([cbc_tok] if cbc_tok else []))
                    bstate[b] = tev
                    cb_toks.append(tev)
                mtoks, cbc_tok = moe_expert_block(
                    p, ws, hbuf, u_sc, cbc[:, :], rhs_xn, spans, banks, bstate,
                    None, _kt_view(w1T[ex_i]), _kt_view(w2T[ex_i]),
                    IT, KT, IT, KT, macc, False, xn_tok,
                    cb_toks[-1], u_cons, macc_guard=None,
                    b1col=lambda mt, e=ex_i: b1_sb[:, e * IT + mt: e * IT + mt + 1])
                macc_toks += mtoks
            g1_toks = []
            gprev = None
            for kt in range(KT):
                gprev = p.op("scalar", lambda E, o=macc[:, kt * TC:(kt + 1) * TC],
                             i=macc[:, kt * TC:(kt + 1) * TC]:
                             E.activation(o, i, AF.Silu),
                             waits=macc_toks[-2 * nsp:] + ([gprev] if gprev else []) +
                             [macc_toks[kt]] if False else
                             macc_toks + ([gprev] if gprev else []))
                g1_toks.append(gprev)
            ph2 = p.tokens()

        with nc.sbuf_tensor("x1_sc", [128, 2 * TC], F32) as x1_sc, \
             nc.sbuf_tensor("x2_sc", [128, 2 * TC], F32) as x2_sc:
            p.barrier(ph2)
            s_x1 = [p.sem(f"s_x1{i}") for i in range(2)]
            s_st = [p.sem(f"s_st{i}") for i in range(2)]
            prevx = {}
            for kt in range(KT):
                pb = kt % 2
                wx = prevx.get(pb, [])
                tx1 = p.dma("sync", s_x1[pb], x1_sc[:, pb * TC:(pb + 1) * TC],
                            _kt_view(x1)[:, kt, :], waits=wx)
                tm = p.op("vector", lambda E, o=x2_sc[:, pb * TC:(pb + 1) * TC],
                          a=s1_sb[:, kt * TC:(kt + 1) * TC],
                          bq=macc[:, kt * TC:(kt + 1) * TC]:
                          E.tensor_tensor(o, a, bq, OP.mult),
                          waits=[g1_toks[kt]] + wx)
                ta = p.op("vector", lambda E, o=x2_sc[:, pb * TC:(pb + 1) * TC],
                          a=x2_sc[:, pb * TC:(pb + 1) * TC],
                          bq=x1_sc[:, pb * TC:(pb + 1) * TC]:
                          E.tensor_tensor(o, a, bq, OP.add), waits=[tm, tx1])
                tst = p.dma("sync", s_st[pb], _kt_view(o_x2)[:, kt, :],
                            x2_sc[:, pb * TC:(pb + 1) * TC], waits=[ta])
                prevx[pb] = [tst, ta]
            p.wait_only("gpsimd", (s_st[0], s_st[0].n))
            p.wait_only("gpsimd", (s_st[1], s_st[1].n))

        blk = stack.enter_context(nc.Block())
        p.run(blk)
    return nc


def _pm(vec, nt):
    return np.ascontiguousarray(np.asarray(vec, np.float32).reshape(nt, 128).T)


def kernel(**inputs):
    CC = np.ascontiguousarray
    inp = {k: np.asarray(v) for k, v in inputs.items()}
    x0 = inp["L0_attn_hidden_states"].reshape(TOKENS, D)
    KT = D // 128
    ET = E // 128
    IT = IEXP // 128

    winT = CC(inp["L0_attn_in_proj_w"].T)
    winT_u = CC(winT[:, :E])
    winT_g = CC(winT[:, E:])
    convw = CC(inp["L0_attn_conv_w"].reshape(E, KCONV)
               .reshape(ET, 128, KCONV).transpose(1, 0, 2).reshape(128, ET * KCONV))
    convb = _pm(inp["L0_attn_conv_b"], ET)
    xprojT = CC(inp["L0_attn_x_proj_w"].T)
    dtprojT = CC(inp["L0_attn_dt_proj_w"].T)
    dtb = _pm(inp["L0_attn_dt_proj_b"], ET)
    in_maps = []
    for c in range(NCORE):
        r0 = c * TC
        x0h = np.zeros((D, TC + 3), np.float32)
        x0h[:, 3:] = x0[r0:r0 + TC].T
        if c % 4 != 0:
            x0h[:, :3] = x0[r0 - 3:r0].T
        in_maps.append({
            "x0h": CC(x0h), "winT_u": winT_u, "winT_g": winT_g,
            "convw": convw, "convb": convb, "xprojT": xprojT,
            "dtprojT": dtprojT, "dtb": dtb,
        })
    resA = run_bass_kernel_spmd(build_launch_A(), in_maps, core_ids=list(range(NCORE)))
    A = resA.results

    xc_full = np.concatenate([A[c]["o_xc"] for c in range(NCORE)], axis=1)
    dt_full = np.concatenate([A[c]["o_dt"] for c in range(NCORE)], axis=1)
    bcrow = np.concatenate([A[c]["o_bc"] for c in range(NCORE)], axis=1)
    alog_f = inp["L0_attn_A_log"]
    dv_f = inp["L0_attn_D"]
    in_maps = []
    for c in range(NCORE):
        e0 = c * ES
        ETS = ES // 128
        alog_c = CC(alog_f[e0:e0 + ES].reshape(ETS, 128, NST)
                    .transpose(1, 0, 2).reshape(128, ETS * NST))
        in_maps.append({
            "xc_e": CC(xc_full[e0:e0 + ES]), "dt_e": CC(dt_full[e0:e0 + ES]),
            "bcrow": bcrow, "alog": alog_c, "dvec": _pm(dv_f[e0:e0 + ES], ETS),
        })
    resB = run_bass_kernel_spmd(build_launch_B(), in_maps, core_ids=list(range(NCORE)))
    B = resB.results

    y_full = np.concatenate([B[c]["o_y"] for c in range(NCORE)], axis=0)
    woutT0 = CC(inp["L0_attn_out_proj_w"].T)
    gatewT0 = CC(inp["L0_ffn_gate_w"].T)
    shgT = CC(inp["L0_ffn_shared_gate"].T)
    shuT = CC(inp["L0_ffn_shared_up"].T)
    shdT = CC(inp["L0_ffn_shared_down"].T)
    egT = CC(inp["L0_ffn_expert_gate"].transpose(0, 2, 1))
    euT = CC(inp["L0_ffn_expert_up"].transpose(0, 2, 1))
    edT = CC(inp["L0_ffn_expert_down"].transpose(0, 2, 1))
    in_maps = []
    for c in range(NCORE):
        r0 = c * TC
        in_maps.append({
            "y_t": CC(y_full[:, r0:r0 + TC]), "sg": A[c]["o_sg"],
            "xn0": A[c]["o_xn"], "x0": CC(x0[r0:r0 + TC].T),
            "woutT": woutT0, "gatewT": gatewT0, "shgT": shgT, "shuT": shuT,
            "shdT": shdT, "egT": egT, "euT": euT, "edT": edT,
        })
    resC = run_bass_kernel_spmd(build_launch_C(), in_maps, core_ids=list(range(NCORE)))
    C = resC.results

    wqkvT = inp["L1_attn_Wqkv"].T
    bqkv_f = inp["L1_attn_b_qkv"]
    slopes = (1.0 / 2.0 ** (np.arange(1, NHEAD + 1) * (8.0 / NHEAD))).astype(np.float32)
    pos = np.arange(1 - SEQ, 1, dtype=np.float32)
    KTT = SEQ // 128
    in_maps = []
    for c in range(NCORE):
        b = c // 4
        heads = [(c % 4) * 4 + i for i in range(4)]
        cols = []
        bias_m = []
        for part in range(3):
            for h in heads:
                cols.append(wqkvT[:, part * D + h * HDIM: part * D + (h + 1) * HDIM])
                bias_m.append(bqkv_f[part * D + h * HDIM: part * D + (h + 1) * HDIM])
        wq_c = np.concatenate(cols, axis=1)
        KTD = D // 128
        wq_c = CC(wq_c.reshape(KTD, 128, 12, 128).transpose(1, 2, 0, 3)
                  .reshape(128, 12 * KTD * 128))
        bq_c = CC(np.stack(bias_m, 0).T)
        xn1b = CC(np.concatenate([C[4 * b + g]["o_xn1"] for g in range(4)], axis=1))
        alibi_c = CC(pos[None, :] * slopes[heads][:, None])
        acolT = np.zeros((128, 4 * KTT), np.float32)
        for i in range(4):
            for kt in range(KTT):
                acolT[:, i * KTT + kt] = alibi_c[i, kt * 128:(kt + 1) * 128]
        in_maps.append({
            "xn1b": xn1b, "wqkvT": wq_c, "bqkv": bq_c,
            "alibi": alibi_c, "alibi_colT": acolT,
        })
    resD = run_bass_kernel_spmd(build_launch_D(), in_maps, core_ids=list(range(NCORE)))
    Dr = resD.results

    woutT1 = CC(inp["L1_attn_Wout"].T)
    bout = _pm(inp["L1_attn_b_out"], KT)
    gatewT1 = CC(inp["L1_ffn_gate_w"].T)
    gateb = CC(inp["L1_ffn_gate_b"].reshape(1, NEXP))
    ebias = CC(inp["L1_ffn_bias"].reshape(1, NEXP))
    w1T = CC(inp["L1_ffn_w1"].transpose(0, 2, 1))
    b1f = inp["L1_ffn_b1"]
    b1 = np.zeros((128, NEXP * IT), np.float32)
    for e in range(NEXP):
        for mt in range(IT):
            b1[:, e * IT + mt] = b1f[e, mt * 128:(mt + 1) * 128]
    w2T = CC(inp["L1_ffn_w2"].transpose(0, 2, 1))
    b2 = CC(inp["L1_ffn_b2"])
    ctxb = [np.concatenate([Dr[4 * b + g]["o_ctx"] for g in range(4)], axis=0)
            for b in range(BATCH)]
    in_maps = []
    for c in range(NCORE):
        b = c // 4
        t0 = (c % 4) * TC
        in_maps.append({
            "ctx": CC(ctxb[b][:, t0:t0 + TC]), "x1": C[c]["o_x1"],
            "xn1": C[c]["o_xn1"], "woutT": woutT1, "bout": bout,
            "gatewT": gatewT1, "gateb": gateb, "ebias": ebias,
            "w1T": w1T, "b1": b1, "w2T": w2T, "b2": b2,
        })
    resE = run_bass_kernel_spmd(build_launch_E(), in_maps, core_ids=list(range(NCORE)))
    Er = resE.results

    out = np.concatenate([np.asarray(Er[c]["o_x2"]).T for c in range(NCORE)], axis=0)
    return out.reshape(BATCH, SEQ, D).astype(np.float32)
